# version 36
# baseline (speedup 1.0000x reference)
import numpy as np
from contextlib import ExitStack

import concourse.bass as bass
import concourse.mybir as mybir
from concourse.bass_utils import run_bass_kernel_spmd

F32 = mybir.dt.float32
BF16 = mybir.dt.bfloat16
I32 = mybir.dt.int32
AF = mybir.ActivationFunctionType
ALU = mybir.AluOpType
AX = mybir.AxisListType

NCORE = 8
BPC = 2
S = 2048
D = 1024
RC = 12582912.0
EXP_SHIFT = -15.0

BASE = 20480
OFF_WA = BASE + 0
OFF_WB = BASE + 16384
OFF_X = BASE + 32768
OFF_Q = BASE + 65536
OFF_K = BASE + 98304
OFF_V = BASE + 131072
OFF_Z = BASE + 131072
OFF_X1B = BASE + 139264
OFF_QI = BASE + 147456
OFF_PT = BASE + 164864
OFF_XSG = BASE + 173056
OFF_OS = BASE + 181248
OFF_BB = BASE + 197632
OFF_SM = BASE + 205824


def build_nc():
    nc = bass.Bass("TRN2", target_bir_lowering=False)

    x_h = nc.dram_tensor("x", [BPC, S, D], F32, kind="ExternalInput")
    xt_h = nc.dram_tensor("xt", [BPC, D, S], F32, kind="ExternalInput")
    wq_h = nc.dram_tensor("wqt", [D, D], F32, kind="ExternalInput")
    wk_h = nc.dram_tensor("wkt", [D, D], F32, kind="ExternalInput")
    wv_h = nc.dram_tensor("wvt", [D, D], F32, kind="ExternalInput")
    wo_h = nc.dram_tensor("wot", [D, D], F32, kind="ExternalInput")
    w8_h = nc.dram_tensor("w8t", [D, D], I32, kind="ExternalInput")
    ws_h = nc.dram_tensor("wsc", [1, 1], F32, kind="ExternalInput")
    bias_h = nc.dram_tensor("bias", [1, D], F32, kind="ExternalInput")
    y_h = nc.dram_tensor("y", [BPC, S, D], F32, kind="ExternalOutput")

    with ExitStack() as ctx:
        e = ctx.enter_context

        def at(name, shape, dtype, off):
            return nc.alloc_sbuf_tensor_at(name, shape, dtype, offset=off)

        wslot = [at("wslotA", [128, 8192], BF16, OFF_WA),
                 at("wslotB", [128, 8192], BF16, OFF_WB)]
        xT = at("xT", [128, 16384], BF16, OFF_X)
        qT = at("qT", [128, 16384], BF16, OFF_X)
        o_ring = at("o_ring", [128, 8192], BF16, OFF_OS)
        QT = at("QT", [128, 16384], BF16, OFF_Q)
        OT = at("OT", [128, 16384], BF16, OFF_Q)
        KT = at("KT", [128, 16384], BF16, OFF_K)
        x1 = at("x1", [128, 16384], BF16, OFF_K)
        Vx = at("Vx", [128, 16896], BF16, OFF_V)
        z2 = at("z2", [128, 2048], F32, OFF_Z)
        z2x = at("z2x", [128, 2048], F32, OFF_V + 20480)
        dump = at("dump", [128, 1024], BF16, OFF_V + 28672)
        dumpb = at("dumpb", [128, 1024], BF16, OFF_V + 30720)
        x1b2 = at("x1b2", [128, 2048], F32, OFF_X1B)
        qint2 = at("qint2", [128, 2048], BF16, OFF_QI)
        PT = at("PT", [128, 4096], BF16, OFF_PT)
        xstg = at("xstg", [128, 2048], F32, OFF_XSG)
        outst2 = at("outst2", [128, 4096], F32, OFF_OS)
        bias_b2 = at("bias_b2", [128, 2048], F32, OFF_BB)

        o = OFF_SM
        def small(name, shape, dtype):
            nonlocal o
            t = at(name, shape, dtype, o)
            nbytes = shape[1] * {F32: 4, BF16: 2, I32: 4}[dtype]
            o += (nbytes + 31) // 32 * 32
            return t
        ident = small("ident", [128, 128], BF16)
        mask01 = small("mask01", [128, 128], BF16)
        iota_i = small("iota_i", [128, 128], I32)
        ones_f = small("ones_f", [1, 128], F32)
        ws_sb = small("ws_sb", [1, 1], F32)
        bias_sb = at("bias_sb", [1, 1024], F32, OFF_OS)
        ebias = small("ebias", [128, 1], F32)
        zcol = small("zcol", [128, 1], F32)
        rcol = small("rcol", [128, 1], F32)
        nrcol = small("nrcol", [128, 1], F32)
        wsb127 = small("wsb127", [128, 1], F32)
        st12 = small("st12", [128, 24], F32)
        ag4 = small("ag4", [128, 8], F32)
        sc = small("sc", [128, 68], F32)
        rinv = small("rinv", [128, 8], F32)
        w2a = small("w2a", [128, 16], F32)
        st12b = small("st12b", [128, 24], F32)
        ag4b = small("ag4b", [128, 4], F32)
        assert o <= 229000, o

        psall = e(nc.psum_tensor("psall", [128, 4096], F32))
        stbuf = [lambda p=p: psall[:, p * 1024:(p + 1) * 1024]
                 for p in range(2)]
        oA = psall[:, 2048:2308]
        oB = psall[:, 2560:2820]
        pp = psall[:, 3072:4096]
        bufY = [psall[:, 0:2048], psall[:, 2048:4096]]

        sem_d = e(nc.semaphore("sem_d"))
        sem_g = e(nc.semaphore("sem_g"))
        sem_o = e(nc.semaphore("sem_o"))
        sem_t = e(nc.semaphore("sem_t"))
        sem_s = e(nc.semaphore("sem_s"))
        sem_v = e(nc.semaphore("sem_v"))
        sem_m = e(nc.semaphore("sem_m"))
        sem_gx = e(nc.semaphore("sem_gx"))
        sem_gb = e(nc.semaphore("sem_gb"))
        sem_o2 = e(nc.semaphore("sem_o2"))
        sem_xd = e(nc.semaphore("sem_xd"))
        sem_xd2 = e(nc.semaphore("sem_xd2"))
        block = e(nc.Block())

        ops = {k: [] for k in ("t", "v", "s", "g", "y")}
        ppu = {"n": 0, "ev": {}}
        ctr = {k: 0 for k in ("t", "v", "s", "g", "d", "gd", "o",
                              "gx", "gb", "o2", "xd", "xd2")}
        SEM = {"t": sem_t, "v": sem_v, "s": sem_s, "g": sem_m,
               "d": sem_d, "gd": sem_g, "o": sem_o, "gx": sem_gx,
               "gb": sem_gb, "o2": sem_o2, "xd": sem_xd, "xd2": sem_xd2}

        def emit(engine, fn):
            ops[engine].append(fn)

        def bump(key, n=1):
            ctr[key] += n
            return ctr[key]

        def W(engine, key, val):
            if val is None or val <= 0:
                return
            sem = SEM[key]
            emit(engine, lambda eng, sem=sem, val=val: eng.wait_ge(sem, val))

        def ap(t, off, *dims):
            return bass.AP(t, off, [list(d) for d in dims])

        def ph0():
            def f(eng):
                eng.dma_start(ws_sb[:, :], ws_h.ap()).then_inc(sem_d, 16)
                eng.dma_start(bias_sb[:, :], bias_h.ap()).then_inc(sem_d, 16)
            emit("y", f)
            d_ws = bump("d", 32)

            def f(eng):
                eng.iota(iota_i[:, :], [[1, 128]],
                         channel_multiplier=-1).then_inc(sem_m, 1)
            emit("g", f)
            g_iota = bump("g")

            W("v", "g", g_iota)
            def f(eng):
                eng.tensor_single_scalar(ident[:, :], iota_i[:, :], 0.0,
                                         ALU.is_equal)
                eng.tensor_single_scalar(mask01[:, :], iota_i[:, :], 0.0,
                                         ALU.is_ge)
                eng.memset(ebias[:, :], EXP_SHIFT)
                eng.memset(zcol[:, :], 0.0)
                eng.memset(rcol[:, :], RC)
                eng.memset(nrcol[:, :], -RC)
                eng.memset(ones_f[:, :], 1.0)
                eng.memset(psall[:, :], 0.0).then_inc(sem_v, 1)
            emit("v", f)
            v_const = bump("v")

            W("t", "d", d_ws)
            W("t", "v", v_const)
            def f(eng):
                eng.matmul(pp[:, 0:1], ones_f[:, :], ws_sb[:, :],
                           start=True, stop=True, skip_group_check=True)
                eng.matmul(pp[:, 512:1024], ones_f[:, :], bias_sb[:, 0:512],
                           start=True, stop=True,
                           skip_group_check=True).then_inc(sem_t, 1)
            emit("t", f)
            t_bc1 = bump("t")

            W("v", "t", t_bc1)
            def f(eng):
                eng.tensor_scalar(wsb127[:, :], pp[:, 0:1], 1.0 / 127.0,
                                  None, ALU.mult)
                eng.tensor_copy(bias_b2[:, 0:512], pp[:, 512:1024])
                eng.tensor_copy(bias_b2[:, 1024:1536],
                                pp[:, 512:1024]).then_inc(sem_v, 1)
            emit("v", f)
            v_bc1 = bump("v")

            W("t", "v", v_bc1)
            def f(eng):
                eng.matmul(pp[:, 512:1024], ones_f[:, :],
                           bias_sb[:, 512:1024], start=True, stop=True,
                           skip_group_check=True).then_inc(sem_t, 1)
            emit("t", f)
            t_bc2 = bump("t")

            W("v", "t", t_bc2)
            def f(eng):
                eng.tensor_copy(bias_b2[:, 512:1024], pp[:, 512:1024])
                eng.tensor_copy(bias_b2[:, 1536:2048],
                                pp[:, 512:1024]).then_inc(sem_v, 1)
            emit("v", f)
            return bump("v")

        def ph1(b, v_ready, prev_t_end, prev_v_end, prev_s_end, prev_m_end):
            W("g", "t", prev_t_end)
            for c in range(4):
                def f(eng, c=c, b=b):
                    src = ap(xt_h, b * D * S + (2 * c) * 128 * S,
                             (S, 128), (128 * S, 2), (1, S))
                    dst = ap(xT, (2 * c) * 2048, (16384, 128), (2048, 2),
                             (1, 2048))
                    eng.dma_start(dst, src).then_inc(sem_gx, 16)
                emit("g", f)
                bump("gx", 16)
            xt_done = ctr["gx"]

            def wload(w_handle, slot, wait_t):
                W("g", "t", wait_t)
                key = "gd" if slot == 0 else "gb"
                sem_w = sem_g if slot == 0 else sem_gb
                def f(eng, w_handle=w_handle, slot=slot, sem_w=sem_w):
                    src = ap(w_handle, 0, (D, 128), (128 * D, 8), (1, D))
                    dst = ap(wslot[slot], 0, (8192, 128), (1024, 8),
                             (1, 1024))
                    eng.dma_start(dst, src).then_inc(sem_w, 16)
                emit("g", f)
                return (key, bump(key, 16))

            wk_done = wload(wk_h, 1, prev_t_end)
            wq_done = wload(wq_h, 0, prev_t_end)

            evK = {}
            evQ = {}
            H = {"half": 0, "first_ev": True, "q_all": None}
            slotc = [None, None, None, None]

            def pslot(i):
                return psall[:, (i % 4) * 512:((i % 4) + 1) * 512]

            def proj_group(slot, dt, sch, dest, ev):
                j = H["half"] % 4
                if slotc[j] is not None:
                    W("t", slotc[j][0], slotc[j][1])
                def f(eng, dt=dt, sch=sch, j=j, slot=slot):
                    out = pslot(j)
                    for kt in range(8):
                        lhsT = wslot[slot][
                            :, kt * 1024 + dt * 128:
                            kt * 1024 + dt * 128 + 128]
                        rhs = xT[:, kt * 2048 + sch * 512:
                                 kt * 2048 + (sch + 1) * 512]
                        mm = eng.matmul(out, lhsT, rhs,
                                        start=(kt == 0),
                                        stop=(kt == 7),
                                        skip_group_check=True)
                        if kt == 7:
                            mm.then_inc(sem_t, 1)
                emit("t", f)
                mmv = bump("t")
                W("v", "t", mmv)
                if H["first_ev"]:
                    H["first_ev"] = False
                    W("v", "t", prev_t_end)
                    W("v", "g", prev_m_end)
                def f3(eng, dt=dt, sch=sch, j=j, dest=dest):
                    eng.tensor_copy(dest[:, dt * 2048 + sch * 512:
                                         dt * 2048 + (sch + 1) * 512],
                                    pslot(j)).then_inc(sem_v, 1)
                emit("v", f3)
                ev[(dt, sch)] = bump("v")
                slotc[j] = ("v", ev[(dt, sch)])
                H["half"] += 1

            W("t", wk_done[0], wk_done[1])
            W("t", "gx", xt_done)
            W("t", "v", v_ready)
            W("t", "v", prev_v_end)
            W("t", "s", prev_s_end)
            for sch in range(4):
                for dt in range(8):
                    proj_group(1, dt, sch, KT, evK)
            k_last_mm = ctr["t"]
            wv_done = wload(wv_h, 1, k_last_mm)

            W("t", wq_done[0], wq_done[1])
            for dt in range(8):
                proj_group(0, dt, 0, QT, evQ)
            qrem = [(dt, sch) for sch in (1, 2, 3) for dt in range(8)]
            qst = {"i": 0}

            def emit_qgroup():
                dt, sch = qrem[qst["i"]]
                qst["i"] += 1
                proj_group(0, dt, sch, QT, evQ)
                if qst["i"] == len(qrem):
                    H["q_all"] = ctr["t"]

            W("v", "s", prev_s_end)
            def f(eng):
                eng.memset(ap(Vx, 64, (16896, 128), (1056, 16), (66, 16),
                              (1, 1)), 1.0).then_inc(sem_v, 1)
            emit("v", f)
            vones = bump("v")

            v_upto = {}

            def emit_vgroup(g):
                st, dc = g // 2, g % 2
                n = ppu["n"]
                ppu["n"] += 1
                phh = n % 2
                if g < 2:
                    W("t", wv_done[0], wv_done[1])
                W("t", "v", ppu["ev"].get(n - 2))
                def f(eng, st=st, dc=dc, phh=phh):
                    out = pp[:, phh * 512:(phh + 1) * 512]
                    for kt in range(8):
                        lhsT = xT[:, kt * 2048 + st * 128:
                                  kt * 2048 + st * 128 + 128]
                        rhs = wslot[1][:, kt * 1024 + dc * 512:
                                       kt * 1024 + (dc + 1) * 512]
                        mm = eng.matmul(out, lhsT, rhs, start=(kt == 0),
                                        stop=(kt == 7),
                                        skip_group_check=True)
                        if kt == 7:
                            mm.then_inc(sem_t, 1)
                emit("t", f)
                mmv = bump("t")
                W("v", "t", mmv)
                def f2(eng, st=st, dc=dc, phh=phh):
                    dst = ap(Vx, st * 1056 + dc * 8 * 66,
                             (16896, 128), (66, 8), (1, 64))
                    src = ap(psall, 3072 + phh * 512, (4096, 128),
                             (64, 8), (1, 64))
                    eng.tensor_copy(dst, src).then_inc(sem_v, 1)
                emit("v", f2)
                val = bump("v")
                ppu["ev"][n] = val
                v_upto[st] = val

            return dict(slotc=slotc, evK=evK, evQ=evQ,
                        emit_qgroup=emit_qgroup, qst=qst, nq=len(qrem),
                        H=H, vones=vones, emit_vg=emit_vgroup,
                        v_upto=v_upto)

        def ph2(b, P):
            slotc = P["slotc"]
            evK = P["evK"]
            evQ = P["evQ"]
            emit_qgroup = P["emit_qgroup"]
            qst = P["qst"]
            nq = P["nq"]
            vones = P["vones"]
            emit_vg = P["emit_vg"]
            v_upto = P["v_upto"]
            def wload2(w_handle, slot, wait_t):
                W("g", "t", wait_t)
                key = "gd" if slot == 0 else "gb"
                sem_w = sem_g if slot == 0 else sem_gb
                def f(eng, w_handle=w_handle, slot=slot, sem_w=sem_w):
                    src = ap(w_handle, 0, (D, 128), (128 * D, 8), (1, D))
                    dst = ap(wslot[slot], 0, (8192, 128), (1024, 8),
                             (1, 1024))
                    eng.dma_start(dst, src).then_inc(sem_w, 16)
                emit("g", f)
                return (key, bump(key, 16))
            w8_done = None
            wo_done = None

            expv = {}
            maskv = {}
            exp_last = {0: 0, 1: 0}
            trans_ev = {}
            u = 0
            evac_all = {}
            trans_last_t = {}
            vg_next = 0
            prev_evac = vones

            def emit_st(qc, hp, k, p):
                for j in (2 * p, 2 * p + 1):
                    if slotc[j] is not None:
                        W("t", slotc[j][0], slotc[j][1])
                W("t", "v", evK[(hp, k // 4)])
                W("t", "v", evQ[(hp, qc)])
                def f(eng, qc=qc, hp=hp, k=k, p=p):
                    j = k - 4 * qc
                    colo = max(0, j) * 128
                    last = None
                    for s_ in range(2):
                        p0 = s_ * 64
                        lhsT = KT[p0:p0 + 64, hp * 2048 + k * 128:
                                  hp * 2048 + k * 128 + 128]
                        rhs = QT[p0:p0 + 64, hp * 2048 + qc * 512 + colo:
                                 hp * 2048 + (qc + 1) * 512]
                        out = psall[:, p * 1024 + s_ * 512 + colo:
                                    p * 1024 + (s_ + 1) * 512]
                        last = eng.matmul(out, lhsT, rhs, start=True,
                                          stop=True, skip_group_check=True)
                    last.then_inc(sem_t, 1)
                emit("t", f)
                return bump("t")

            def emit_pv(qc, hp, mms, n_started, n_total):
                def f(eng, hp=hp, mms=list(mms),
                      n_started=n_started, n_total=n_total):
                    last = None
                    for idx, (k, pt, qt, s_) in enumerate(mms):
                        h = hp * 2 + s_
                        i_glob = n_started + idx
                        lhsT = PT[:, pt * 1024 + s_ * 512 + qt * 128:
                                  pt * 1024 + s_ * 512 + qt * 128 + 128]
                        rhs = Vx[:, k * 1056 + h * 66:k * 1056 + h * 66 + 65]
                        out = (oA if s_ == 0 else oB)[:, qt * 65:
                                                      qt * 65 + 65]
                        last = eng.matmul(
                            out, lhsT, rhs,
                            start=(i_glob < 2),
                            stop=(i_glob >= n_total - 2),
                            skip_group_check=True)
                    last.then_inc(sem_t, 1)
                emit("t", f)
                return bump("t")

            def emit_trans(qc, i):
                t_ = 4 * qc + i // 2
                dc = i % 2
                n = ppu["n"]
                ppu["n"] += 1
                ph = n % 2
                W("t", "v", evac_all[qc])
                W("t", "v", ppu["ev"].get(n - 2))
                ro = (qc % 2) * 4096
                qt_ = t_ - 4 * qc
                def f(eng, qt_=qt_, dc=dc, ph=ph, ro=ro):
                    for kk in range(4):
                        dt = dc * 4 + kk
                        lhsT = o_ring[:, ro + qt_ * 1024 + dt * 128:
                                      ro + qt_ * 1024 + dt * 128 + 128]
                        out = pp[:, ph * 512 + kk * 128:
                                 ph * 512 + kk * 128 + 128]
                        mm = eng.matmul(out, lhsT, ident[:, :], start=True,
                                        stop=True, skip_group_check=True)
                        if kk == 3:
                            mm.then_inc(sem_t, 1)
                emit("t", f)
                mmv = bump("t")
                trans_last_t[qc] = mmv
                W("v", "t", mmv)
                def f2(eng, t_=t_, dc=dc, ph=ph):
                    dst = ap(OT, dc * 4 * 2048 + t_ * 128,
                             (16384, 128), (2048, 4), (1, 128))
                    eng.tensor_copy(dst, pp[:, ph * 512:(ph + 1) * 512])\
                       .then_inc(sem_v, 1)
                emit("v", f2)
                ppu["ev"][n] = bump("v")
                trans_ev[t_] = ppu["ev"][n]

            for qc in range(4):
                for hp in range(8):
                    if qc >= 1 and hp == 0:
                        while vg_next < min(32, 8 * qc + 8):
                            emit_vg(vg_next)
                            vg_next += 1
                            if vg_next == 32 and w8_done is None:
                                w8_done = wload2(w8_h, 1, ctr["t"])
                    n_kt = 4 * qc + 4
                    n_total = sum(2 * (4 - max(0, kk - 4 * qc))
                                  for kk in range(n_kt))
                    n_started = 0
                    pend = []
                    u0 = u
                    for k in range(n_kt):
                        p = u % 2
                        pt = u % 4
                        if u < 2:
                            W("t", "v", vones)
                        emit_st(qc, hp, k, p)
                        stv = ctr["t"]
                        if k >= 2:
                            ku = k - 2
                            uu = u - 2
                            jd = ku - 4 * qc
                            lo = 0 if jd < 0 else jd + 1
                            mms = pend + [(ku, uu % 4, qt, s_)
                                          for qt in range(lo, 4)
                                          for s_ in range(2)]
                            pend = ([(ku, uu % 4, jd, s_)
                                     for s_ in range(2)] if jd >= 0 else [])
                            if mms:
                                W("t", "g", maskv.get(uu - 1))
                                W("t", "v", v_upto.get(ku))
                                if ku == 0:
                                    W("t", "v", prev_evac)
                                emit_pv(qc, hp, mms, n_started, n_total)
                                n_started += len(mms)
                        W("s", "t", stv)
                        def f2(eng, p=p, pt=pt,
                               colo=max(0, k - 4 * qc) * 128):
                            if colo == 0:
                                eng.activation(
                                    PT[:, pt * 1024:(pt + 1) * 1024],
                                    psall[:, p * 1024:(p + 1) * 1024],
                                    AF.Exp, bias=ebias[:, :],
                                    scale=0.125).then_inc(sem_s, 1)
                            else:
                                w_ = 512 - colo
                                src = ap(psall, p * 1024 + colo,
                                         (4096, 128), (512, 2), (1, w_))
                                dst = ap(PT, pt * 1024 + colo,
                                         (4096, 128), (512, 2), (1, w_))
                                eng.activation(
                                    dst, src, AF.Exp, bias=ebias[:, :],
                                    scale=0.125).then_inc(sem_s, 1)
                        emit("s", f2)
                        expv[u] = bump("s")
                        exp_last[p] = expv[u]
                        slotc[2 * p] = ("s", expv[u])
                        slotc[2 * p + 1] = ("s", expv[u])
                        nst = u + 1
                        if qc == 0:
                            vtarget = min(8, 2 * nst)
                            qtarget = min(8, max(0, nst - 4) // 3)
                        elif qc == 1:
                            u1 = nst - 32
                            vtarget = 8 + min(24, u1 // 2)
                            qtarget = 8 + min(8, u1 // 6)
                        elif qc == 2:
                            u2 = nst - 96
                            vtarget = 32
                            qtarget = 16 + min(8, u2 // 8)
                        else:
                            vtarget = 32
                            qtarget = 24
                        while qst["i"] < qtarget:
                            emit_qgroup()
                        if qst["i"] == nq and wo_done is None:
                            wo_done = wload2(wo_h, 0, P["H"]["q_all"])
                        while vg_next < vtarget:
                            emit_vg(vg_next)
                            vg_next += 1
                            if vg_next == 32 and w8_done is None:
                                w8_done = wload2(w8_h, 1, ctr["t"])
                        if k >= 4 * qc:
                            W("g", "s", expv[u])
                            def f3(eng, qc=qc, k=k, pt=pt):
                                j = k - 4 * qc
                                last = None
                                for s_ in range(2):
                                    off = (pt * 1024 + s_ * 512 + j * 128)
                                    blk = PT[:, off:off + 128]
                                    last = eng.tensor_tensor(
                                        blk, blk, mask01[:, :], ALU.mult)
                                last.then_inc(sem_m, 1)
                            emit("g", f3)
                            maskv[u] = bump("g")
                        else:
                            maskv[u] = ctr["g"]
                        u += 1
                    for ku in (n_kt - 2, n_kt - 1):
                        uu = u0 + ku
                        jd = ku - 4 * qc
                        lo = 0 if jd < 0 else jd + 1
                        mms = pend + [(ku, uu % 4, qt, s_)
                                      for qt in range(lo, 4)
                                      for s_ in range(2)]
                        pend = ([(ku, uu % 4, jd, s_)
                                 for s_ in range(2)] if jd >= 0 else [])
                        W("t", "s", expv[uu])
                        W("t", "g", maskv.get(uu - 1))
                        if mms:
                            W("t", "v", v_upto.get(ku))
                            emit_pv(qc, hp, mms, n_started, n_total)
                            n_started += len(mms)
                    if pend:
                        W("t", "g", maskv[u0 + n_kt - 1])
                        W("t", "v", v_upto.get(n_kt - 1))
                        emit_pv(qc, hp, pend, n_started, n_total)
                        n_started += len(pend)
                        pend = []
                    assert n_started == n_total, (qc, hp, n_started, n_total)
                    pv_last = ctr["t"]

                    W("v", "t", pv_last)
                    if hp == 0:
                        W("v", "t", trans_last_t.get(qc - 2))
                        if qc == 0:
                            W("v", "o", ctr["o"])
                            W("v", "o2", ctr["o2"])
                    def f5(eng, qc=qc, hp=hp):
                        eng.reciprocal(rinv[:, 0:4],
                                       ap(psall, 2048 + 64, (4096, 128),
                                          (65, 4), (1, 1)))
                        eng.reciprocal(rinv[:, 4:8],
                                       ap(psall, 2560 + 64, (4096, 128),
                                          (65, 4), (1, 1)))
                        eng.drain()
                        last = None
                        ro = (qc % 2) * 4096
                        for s_ in range(2):
                            h = hp * 2 + s_
                            ps = oA if s_ == 0 else oB
                            for qt in range(4):
                                last = eng.tensor_scalar(
                                    o_ring[:, ro + qt * 1024 + h * 64:
                                           ro + qt * 1024 + h * 64 + 64],
                                    ps[:, qt * 65:qt * 65 + 64],
                                    rinv[:, s_ * 4 + qt:s_ * 4 + qt + 1],
                                    None, ALU.mult)
                        last.then_inc(sem_v, 1)
                    emit("v", f5)
                    prev_evac = bump("v")

                    if qc >= 1:
                        emit_trans(qc - 1, hp)
                evac_all[qc] = prev_evac
            for i in range(8):
                emit_trans(3, i)
            return (ppu["ev"][ppu["n"] - 1], wo_done, w8_done,
                    exp_last, trans_ev)

        def ph45(b, trans_done, wo_done, w8_done, gpair, exp_last,
                 trans_ev):
            qte = {}
            qtmm_v = {}
            outv = gpair[2]
            o3v = {}
            outrd = {}
            x1pbv = {}
            wo_y2v = {}
            z_done = {}
            statv = {}
            qv = {}
            rsd = {}
            appd = {}
            ffnv = {}
            stats2v = {}
            appd2 = {}
            NP = 16

            def bufP(t):
                return psall[:, (t % 4) * 1024:((t % 4) + 1) * 1024]

            def zq(t):
                zt = z2 if (t % 4) < 2 else z2x
                return zt[:, (t % 2) * 1024:((t % 2) + 1) * 1024]

            def xh(t):
                return xstg[:, (t % 2) * 1024:((t % 2) + 1) * 1024]

            def x1b2h(t):
                return x1b2[:, (t % 2) * 1024:((t % 2) + 1) * 1024]

            def qih(t):
                return qint2[:, (t % 2) * 1024:((t % 2) + 1) * 1024]

            def emit_wo(t):
                if t == 0:
                    W("t", wo_done[0], wo_done[1])
                if t < 4:
                    W("t", "v", trans_ev[t])
                    if t == 0:
                        W("t", "s", exp_last[0])
                    elif t == 1:
                        W("t", "s", exp_last[1])
                    else:
                        W("t", "v", trans_done)
                else:
                    W("t", "s", qte.get(t - 4))
                    W("t", "v", o3v.get(t - 4))
                def f(eng, t=t):
                    for dc in range(2):
                        out = bufP(t)[:, dc * 512:(dc + 1) * 512]
                        for dt in range(8):
                            lhsT = OT[:, dt * 2048 + t * 128:
                                      dt * 2048 + t * 128 + 128]
                            rhs = wslot[0][:, dt * 1024 + dc * 512:
                                           dt * 1024 + (dc + 1) * 512]
                            mm = eng.matmul(out, lhsT, rhs,
                                            start=(dt == 0),
                                            stop=(dt == 7),
                                            skip_group_check=True)
                            if dc == 1 and dt == 7:
                                mm.then_inc(sem_t, 1)
                emit("t", f)
                wo_y2v[t] = bump("t")

            def emit_xdma(t):
                gp = gpair[0]
                gpair[0] += 1
                W("y", "v", gpair[1].get(gp - 2))
                key = "xd" if t % 2 == 0 else "xd2"
                sem_x = SEM[key]
                def fd(eng, t=t, b=b, sem_x=sem_x):
                    src = ap(x_h, b * S * D + t * 128 * D, (D, 128), (1, D))
                    eng.dma_start(xh(t), src).then_inc(sem_x, 16)
                emit("y", fd)
                return gp, key, bump(key, 16)

            def VA(t, gp, x_key, x_dma):
                W("v", "t", wo_y2v[t])
                W("v", x_key, x_dma)
                W("v", "s", outrd.get(t - 4))
                z = zq(t)
                c0 = 14 * (t % 4)
                g2 = 2 * (t % 4)
                sh = 12 * (t % 2)
                def f2(eng, t=t, z=z):
                    eng.scalar_tensor_tensor(z[:, :], bufP(t)[:, :], 1.0,
                                             xh(t), ALU.mult,
                                             ALU.add).then_inc(sem_v, 1)
                    eng.drain()
                emit("v", f2)
                z_done[t] = bump("v")
                gpair[1][gp] = z_done[t]
                def f3(eng, z=z, c0=c0, g2=g2, sh=sh):
                    eng.tensor_scalar(dump[:, :], z[:, :],
                                      1.0, None, ALU.mult, ALU.max,
                                      accum_out=sc[:, c0 + 4:c0 + 5])
                    eng.tensor_scalar(dumpb[:, :], z[:, :],
                                      -1.0, None, ALU.mult, ALU.max,
                                      accum_out=sc[:, c0 + 5:c0 + 6])
                    eng.bn_stats(st12[:, sh:sh + 6], z[:, 0:512])
                    eng.bn_stats(st12[:, sh + 6:sh + 12], z[:, 512:1024])
                    eng.drain()
                    eng.bn_aggr(ag4[:, g2:g2 + 2], st12[:, sh:sh + 12])
                    eng.drain()
                    mean1 = ag4[:, g2:g2 + 1]
                    eng.tensor_scalar(sc[:, c0:c0 + 1],
                                      ag4[:, g2 + 1:g2 + 2],
                                      1e-5, None, ALU.add)
                    eng.tensor_scalar(sc[:, c0 + 8:c0 + 9], mean1, -1.0,
                                      None, ALU.mult)
                    eng.tensor_tensor(sc[:, c0 + 6:c0 + 7],
                                      sc[:, c0 + 4:c0 + 5], mean1,
                                      ALU.subtract)
                    eng.tensor_tensor(sc[:, c0 + 7:c0 + 8],
                                      sc[:, c0 + 5:c0 + 6], mean1, ALU.add)
                    eng.drain()
                    eng.tensor_tensor(sc[:, c0 + 6:c0 + 7],
                                      sc[:, c0 + 6:c0 + 7],
                                      sc[:, c0 + 7:c0 + 8], ALU.max)
                    eng.drain()
                    eng.tensor_tensor(sc[:, c0 + 6:c0 + 7],
                                      sc[:, c0 + 6:c0 + 7],
                                      sc[:, c0 + 6:c0 + 7], ALU.mult)
                    eng.tensor_scalar(sc[:, c0 + 7:c0 + 8],
                                      sc[:, c0:c0 + 1], 1e-10,
                                      None, ALU.mult)
                    eng.drain()
                    eng.tensor_tensor(sc[:, c0 + 1:c0 + 2],
                                      sc[:, c0 + 6:c0 + 7],
                                      sc[:, c0 + 7:c0 + 8], ALU.max)
                    eng.drain()
                    eng.reciprocal(sc[:, c0 + 2:c0 + 4],
                                   sc[:, c0:c0 + 2]).then_inc(sem_v, 1)
                emit("v", f3)
                statv[t] = bump("v")

            def SA(t):
                W("s", "v", statv[t])
                c0 = 14 * (t % 4)
                def f4(eng, t=t, c0=c0):
                    eng.activation(sc[:, c0 + 9:c0 + 11],
                                   sc[:, c0 + 2:c0 + 4], AF.Sqrt)
                    eng.activation(sc[:, c0 + 11:c0 + 12],
                                   sc[:, c0 + 1:c0 + 2], AF.Sqrt)
                    eng.drain()
                    eng.activation(sc[:, c0 + 6:c0 + 7],
                                   sc[:, c0 + 8:c0 + 9], AF.Identity,
                                   scale=sc[:, c0 + 9:c0 + 10])
                    eng.activation(sc[:, c0 + 7:c0 + 8],
                                   sc[:, c0 + 10:c0 + 11], AF.Identity,
                                   scale=127.0)
                    eng.activation(sc[:, c0 + 13:c0 + 14],
                                   sc[:, c0 + 11:c0 + 12], AF.Identity,
                                   scale=sc[:, c0 + 9:c0 + 10])
                    eng.drain()
                    eng.activation(sc[:, c0 + 12:c0 + 13],
                                   sc[:, c0 + 8:c0 + 9], AF.Identity,
                                   scale=sc[:, c0 + 7:c0 + 8])
                    eng.activation(w2a[:, t:t + 1],
                                   sc[:, c0 + 13:c0 + 14], AF.Identity,
                                   scale=wsb127[:, :]).then_inc(sem_s, 1)
                emit("s", f4)
                rsd[t] = bump("s")

            def SB(t):
                c0 = 14 * (t % 4)
                def f6(eng, t=t, c0=c0, z=zq(t)):
                    eng.activation(
                        x1[:, t * 1024:(t + 1) * 1024], z[:, :],
                        AF.Identity, bias=sc[:, c0 + 6:c0 + 7],
                        scale=sc[:, c0 + 9:c0 + 10]).then_inc(sem_s, 1)
                emit("s", f6)
                appd[t] = bump("s")

            def VC(t):
                W("v", "s", rsd[t])
                W("v", "t", qtmm_v.get(t - 2))
                c0 = 14 * (t % 4)
                def f7(eng, t=t, c0=c0, z=zq(t)):
                    eng.tensor_scalar(x1b2h(t), z[:, :],
                                      sc[:, c0 + 7:c0 + 8],
                                      sc[:, c0 + 12:c0 + 13],
                                      ALU.mult, ALU.add)
                    eng.drain()
                    eng.tensor_scalar(x1b2h(t), x1b2h(t), RC, -RC,
                                      ALU.add, ALU.add)
                    eng.drain()
                    eng.tensor_scalar(qih(t), x1b2h(t), 127.0,
                                      -128.0, ALU.min,
                                      ALU.max).then_inc(sem_v, 1)
                emit("v", f7)
                qv[t] = bump("v")

            def G_x1pb(t):
                W("g", "s", appd[t])
                W("g", "v", qv[t])
                def fg2(eng, t=t):
                    eng.tensor_tensor(x1b2h(t),
                                      x1[:, t * 1024:(t + 1) * 1024],
                                      bias_b2[:, 0:1024],
                                      ALU.add).then_inc(sem_m, 1)
                emit("g", fg2)
                x1pbv[t] = bump("g")

            def T_qt_ffn(t):
                W("t", "v", qv[t])
                def f8(eng, t=t):
                    for dt in range(8):
                        lhsT = qih(t)[:, dt * 128:dt * 128 + 128]
                        out = bufP(t)[:, dt * 128:dt * 128 + 128]
                        mm = eng.matmul(out, lhsT, ident[:, :],
                                        start=True, stop=True,
                                        skip_group_check=True)
                        if dt == 7:
                            mm.then_inc(sem_t, 1)
                emit("t", f8)
                qtmm_v[t] = bump("t")
                W("s", "t", qtmm_v[t])
                def f9(eng, t=t):
                    last = None
                    for dc in range(2):
                        dst = ap(qT, dc * 4 * 2048 + t * 128,
                                 (16384, 128), (2048, 4), (1, 128))
                        last = eng.copy(
                            dst, bufP(t)[:, dc * 512:(dc + 1) * 512])
                    last.then_inc(sem_s, 1)
                emit("s", f9)
                qte[t] = bump("s")
                W("t", "s", qte[t])
                if t == 0:
                    W("t", w8_done[0], w8_done[1])
                def f10(eng, t=t):
                    for dc in range(2):
                        out = bufP(t)[:, dc * 512:(dc + 1) * 512]
                        for dt in range(8):
                            lhsT = qT[:, dt * 2048 + t * 128:
                                      dt * 2048 + t * 128 + 128]
                            rhs = wslot[1][:, dt * 1024 + dc * 512:
                                           dt * 1024 + (dc + 1) * 512]
                            mm = eng.matmul(out, lhsT, rhs,
                                            start=(dt == 0),
                                            stop=(dt == 7),
                                            skip_group_check=True)
                            if dc == 1 and dt == 7:
                                mm.then_inc(sem_t, 1)
                emit("t", f10)
                ffnv[t] = bump("t")

            def VD(t):
                W("v", "t", ffnv[t])
                W("v", "g", x1pbv[t])
                z = zq(t)
                sh2 = 12 * (t % 2)
                c2 = 56 + 5 * (t % 2)
                g2b = 2 * (t % 2)
                def f11(eng, t=t, z=z):
                    eng.scalar_tensor_tensor(z[:, :], bufP(t)[:, :],
                                             w2a[:, t:t + 1], x1b2h(t),
                                             ALU.mult,
                                             ALU.add).then_inc(sem_v, 1)
                    eng.drain()
                emit("v", f11)
                o3v[t] = bump("v")
                def f12(eng, z=z, sh2=sh2, c2=c2, g2b=g2b):
                    eng.bn_stats(st12b[:, sh2:sh2 + 6], z[:, 0:512])
                    eng.bn_stats(st12b[:, sh2 + 6:sh2 + 12], z[:, 512:1024])
                    eng.drain()
                    eng.bn_aggr(ag4b[:, g2b:g2b + 2],
                                st12b[:, sh2:sh2 + 12])
                    eng.drain()
                    eng.tensor_scalar(sc[:, c2:c2 + 1],
                                      ag4b[:, g2b + 1:g2b + 2],
                                      1e-5, None, ALU.add)
                    eng.tensor_scalar(sc[:, c2 + 3:c2 + 4],
                                      ag4b[:, g2b:g2b + 1], -1.0, None,
                                      ALU.mult)
                    eng.drain()
                    eng.reciprocal(sc[:, c2 + 1:c2 + 2],
                                   sc[:, c2:c2 + 1]).then_inc(sem_v, 1)
                emit("v", f12)
                stats2v[t] = bump("v")

            def SD(t, gp):
                W("s", "v", stats2v[t])
                if gp >= 2:
                    _ok, _ov = outv[gp - 2]
                    W("s", _ok, _ov)
                c2 = 56 + 5 * (t % 2)
                def f15(eng, t=t, c2=c2, z=zq(t)):
                    eng.activation(sc[:, c2 + 2:c2 + 3],
                                   sc[:, c2 + 1:c2 + 2], AF.Sqrt)
                    eng.activation(z[:, :], z[:, :], AF.Identity,
                                   bias=sc[:, c2 + 3:c2 + 4], scale=1.0)
                    eng.drain()
                    eng.activation(
                        outst2[:, (t % 2) * 1024:(t % 2) * 1024 + 1024],
                        z[:, :], AF.Identity, bias=zcol[:, :],
                        scale=sc[:, c2 + 2:c2 + 3]).then_inc(sem_s, 1)
                emit("s", f15)
                outrd[t] = bump("s")

            def G_out(t, gp):
                W("g", "s", outrd[t])
                osem_ = sem_o if (t % 2) == 0 else sem_o2
                okey = "o" if (t % 2) == 0 else "o2"
                def f16(eng, t=t, b=b, osem_=osem_):
                    dst = ap(y_h, b * S * D + t * 128 * D, (D, 128), (1, D))
                    eng.dma_start(dst,
                                  outst2[:, (t % 2) * 1024:
                                         (t % 2) * 1024 + 1024]
                                  ).then_inc(osem_, 16)
                emit("g", f16)
                bump(okey, 16)
                outv[gp] = (okey, ctr[okey])

            gps = {}
            for t in range(4):
                emit_wo(t)
            for t in range(2):
                gps[t] = emit_xdma(t)
                VA(t, *gps[t])
                SA(t)
                SB(t)
                VC(t)
                G_x1pb(t)
                T_qt_ffn(t)
            for t in (2, 3):
                gps[t] = emit_xdma(t)
                VA(t, *gps[t])
                SA(t)
            for i in range(NP):
                VD(i)
                if i + 4 < NP:
                    emit_wo(i + 4)
                SD(i, gps[i][0])
                G_out(i, gps[i][0])
                if i + 2 < NP:
                    SB(i + 2)
                    VC(i + 2)
                    G_x1pb(i + 2)
                    T_qt_ffn(i + 2)
                if i + 4 < NP:
                    gps[i + 4] = emit_xdma(i + 4)
                    VA(i + 4, *gps[i + 4])
                    SA(i + 4)
            return ctr["v"]

        v_ready = ph0()
        gpair = [0, {}, {}]
        prev_t_end = 0
        prev_v_end = 0
        prev_s_end = 0
        prev_m_end = 0
        for b in range(BPC):
            P = ph1(b, v_ready, prev_t_end, prev_v_end, prev_s_end,
                    prev_m_end)
            (trans_done, wo_done, w8_done, exp_last,
             trans_ev) = ph2(b, P)
            ph45(b, trans_done, wo_done, w8_done, gpair, exp_last,
                 trans_ev)
            prev_t_end = ctr["t"]
            prev_v_end = ctr["v"]
            prev_s_end = ctr["s"]
            prev_m_end = ctr["g"]

        to0, to1 = ctr["o"], ctr["o2"]
        def _final(eng):
            eng.wait_ge(sem_o, to0)
            eng.wait_ge(sem_o2, to1)
        emit("g", _final)

        @block.tensor
        def _(eng):
            for fn in ops["t"]:
                fn(eng)

        @block.vector
        def _(eng):
            for fn in ops["v"]:
                fn(eng)

        @block.scalar
        def _(eng):
            for fn in ops["s"]:
                fn(eng)

        @block.gpsimd
        def _(eng):
            for fn in ops["g"]:
                fn(eng)

        @block.sync
        def _(eng):
            for fn in ops["y"]:
                fn(eng)

    return nc


_NC_CACHE = None


def kernel(L0_attn_x, L0_attn_WQ, L0_attn_WK, L0_attn_WV, L0_attn_WO,
           L0_ffn_w_int8, L0_ffn_weight_scale, L0_ffn_bias):
    global _NC_CACHE
    x = np.ascontiguousarray(np.asarray(L0_attn_x, dtype=np.float32))
    xt = np.ascontiguousarray(np.swapaxes(x, 1, 2))
    wqt = np.ascontiguousarray(np.asarray(L0_attn_WQ, np.float32).T)
    wkt = np.ascontiguousarray(np.asarray(L0_attn_WK, np.float32).T)
    wvt = np.ascontiguousarray(np.asarray(L0_attn_WV, np.float32).T)
    wot = np.ascontiguousarray(np.asarray(L0_attn_WO, np.float32).T)
    w8t = np.ascontiguousarray(np.asarray(L0_ffn_w_int8, np.int32).T)
    wsc = np.asarray(L0_ffn_weight_scale, np.float32).reshape(1, 1)
    bias = np.asarray(L0_ffn_bias, np.float32).reshape(1, D)

    if _NC_CACHE is None:
        _NC_CACHE = build_nc()
    nc = _NC_CACHE

    in_maps = []
    for c in range(NCORE):
        in_maps.append({
            "x": np.ascontiguousarray(x[c * BPC:(c + 1) * BPC]),
            "xt": np.ascontiguousarray(xt[c * BPC:(c + 1) * BPC]),
            "wqt": wqt, "wkt": wkt, "wvt": wvt, "wot": wot,
            "w8t": w8t, "wsc": wsc, "bias": bias,
        })
    res = run_bass_kernel_spmd(nc, in_maps, core_ids=list(range(NCORE)))
    out = np.concatenate([np.asarray(r["y"]) for r in res.results], axis=0)
    return np.ascontiguousarray(out.astype(np.float32))


# version 37
# speedup vs baseline: 1.0387x; 1.0387x over previous
import numpy as np
from contextlib import ExitStack

import concourse.bass as bass
import concourse.mybir as mybir
from concourse.bass_utils import run_bass_kernel_spmd

F32 = mybir.dt.float32
BF16 = mybir.dt.bfloat16
I32 = mybir.dt.int32
AF = mybir.ActivationFunctionType
ALU = mybir.AluOpType
AX = mybir.AxisListType

NCORE = 8
BPC = 2
S = 2048
D = 1024
RC = 12582912.0
EXP_SHIFT = -15.0

BASE = 20480
OFF_WA = BASE + 0
OFF_WB = BASE + 16384
OFF_X = BASE + 32768
OFF_Q = BASE + 65536
OFF_K = BASE + 98304
OFF_V = BASE + 131072
OFF_Z = BASE + 131072
OFF_X1B = BASE + 139264
OFF_QI = BASE + 147456
OFF_PT = BASE + 164864
OFF_XSG = BASE + 173056
OFF_OS = BASE + 181248
OFF_BB = BASE + 197632
OFF_SM = BASE + 205824


def build_nc():
    nc = bass.Bass("TRN2", target_bir_lowering=False)

    x_h = nc.dram_tensor("x", [BPC, S, D], F32, kind="ExternalInput")
    xt_h = nc.dram_tensor("xt", [BPC, D, S], F32, kind="ExternalInput")
    wq_h = nc.dram_tensor("wqt", [D, D], F32, kind="ExternalInput")
    wk_h = nc.dram_tensor("wkt", [D, D], F32, kind="ExternalInput")
    wv_h = nc.dram_tensor("wvt", [D, D], F32, kind="ExternalInput")
    wo_h = nc.dram_tensor("wot", [D, D], F32, kind="ExternalInput")
    w8_h = nc.dram_tensor("w8t", [D, D], I32, kind="ExternalInput")
    ws_h = nc.dram_tensor("wsc", [1, 1], F32, kind="ExternalInput")
    bias_h = nc.dram_tensor("bias", [1, D], F32, kind="ExternalInput")
    y_h = nc.dram_tensor("y", [BPC, S, D], F32, kind="ExternalOutput")

    with ExitStack() as ctx:
        e = ctx.enter_context

        def at(name, shape, dtype, off):
            return nc.alloc_sbuf_tensor_at(name, shape, dtype, offset=off)

        wslot = [at("wslotA", [128, 8192], BF16, OFF_WA),
                 at("wslotB", [128, 8192], BF16, OFF_WB)]
        xT = at("xT", [128, 16384], BF16, OFF_X)
        qT = at("qT", [128, 16384], BF16, OFF_X)
        o_ring = at("o_ring", [128, 8192], BF16, OFF_OS)
        QT = at("QT", [128, 16384], BF16, OFF_Q)
        OT = at("OT", [128, 16384], BF16, OFF_Q)
        KT = at("KT", [128, 16384], BF16, OFF_K)
        x1 = at("x1", [128, 16384], BF16, OFF_K)
        Vx = at("Vx", [128, 16896], BF16, OFF_V)
        z2 = at("z2", [128, 2048], F32, OFF_Z)
        z2x = at("z2x", [128, 2048], F32, OFF_V + 20480)
        dump = at("dump", [128, 1024], BF16, OFF_V + 28672)
        dumpb = at("dumpb", [128, 1024], BF16, OFF_V + 30720)
        x1b2 = at("x1b2", [128, 2048], F32, OFF_X1B)
        qint2 = at("qint2", [128, 2048], BF16, OFF_QI)
        PT = at("PT", [128, 4096], BF16, OFF_PT)
        xstg = at("xstg", [128, 2048], F32, OFF_XSG)
        outst2 = at("outst2", [128, 4096], F32, OFF_OS)
        bias_b2 = at("bias_b2", [128, 2048], F32, OFF_BB)

        o = OFF_SM
        def small(name, shape, dtype):
            nonlocal o
            t = at(name, shape, dtype, o)
            nbytes = shape[1] * {F32: 4, BF16: 2, I32: 4}[dtype]
            o += (nbytes + 31) // 32 * 32
            return t
        ident = small("ident", [128, 128], BF16)
        mask01 = small("mask01", [128, 128], BF16)
        iota_i = small("iota_i", [128, 128], I32)
        ones_f = small("ones_f", [1, 128], F32)
        ws_sb = small("ws_sb", [1, 1], F32)
        bias_sb = at("bias_sb", [1, 1024], F32, OFF_OS)
        ebias = small("ebias", [128, 1], F32)
        zcol = small("zcol", [128, 1], F32)
        rcol = small("rcol", [128, 1], F32)
        nrcol = small("nrcol", [128, 1], F32)
        wsb127 = small("wsb127", [128, 1], F32)
        st12 = small("st12", [128, 24], F32)
        ag4 = small("ag4", [128, 8], F32)
        sc = small("sc", [128, 68], F32)
        rinv = small("rinv", [128, 8], F32)
        w2a = small("w2a", [128, 16], F32)
        st12b = small("st12b", [128, 24], F32)
        ag4b = small("ag4b", [128, 4], F32)
        assert o <= 229000, o

        psall = e(nc.psum_tensor("psall", [128, 4096], F32))
        stbuf = [lambda p=p: psall[:, p * 1024:(p + 1) * 1024]
                 for p in range(2)]
        oA = psall[:, 2048:2308]
        oB = psall[:, 2560:2820]
        pp = psall[:, 3072:4096]
        bufY = [psall[:, 0:2048], psall[:, 2048:4096]]

        sem_d = e(nc.semaphore("sem_d"))
        sem_g = e(nc.semaphore("sem_g"))
        sem_o = e(nc.semaphore("sem_o"))
        sem_t = e(nc.semaphore("sem_t"))
        sem_s = e(nc.semaphore("sem_s"))
        sem_v = e(nc.semaphore("sem_v"))
        sem_m = e(nc.semaphore("sem_m"))
        sem_gx = e(nc.semaphore("sem_gx"))
        sem_gb = e(nc.semaphore("sem_gb"))
        sem_o2 = e(nc.semaphore("sem_o2"))
        sem_xd = e(nc.semaphore("sem_xd"))
        sem_xd2 = e(nc.semaphore("sem_xd2"))
        block = e(nc.Block())

        ops = {k: [] for k in ("t", "v", "s", "g", "y")}
        ppu = {"n": 0, "ev": {}}
        ctr = {k: 0 for k in ("t", "v", "s", "g", "d", "gd", "o",
                              "gx", "gb", "o2", "xd", "xd2")}
        SEM = {"t": sem_t, "v": sem_v, "s": sem_s, "g": sem_m,
               "d": sem_d, "gd": sem_g, "o": sem_o, "gx": sem_gx,
               "gb": sem_gb, "o2": sem_o2, "xd": sem_xd, "xd2": sem_xd2}

        def emit(engine, fn):
            ops[engine].append(fn)

        def bump(key, n=1):
            ctr[key] += n
            return ctr[key]

        def W(engine, key, val):
            if val is None or val <= 0:
                return
            sem = SEM[key]
            emit(engine, lambda eng, sem=sem, val=val: eng.wait_ge(sem, val))

        def ap(t, off, *dims):
            return bass.AP(t, off, [list(d) for d in dims])

        def ph0():
            def f(eng):
                eng.dma_start(ws_sb[:, :], ws_h.ap()).then_inc(sem_d, 16)
                eng.dma_start(bias_sb[:, :], bias_h.ap()).then_inc(sem_d, 16)
            emit("y", f)
            d_ws = bump("d", 32)

            def f(eng):
                eng.iota(iota_i[:, :], [[1, 128]],
                         channel_multiplier=-1).then_inc(sem_m, 1)
            emit("g", f)
            g_iota = bump("g")

            W("v", "g", g_iota)
            def f(eng):
                eng.tensor_single_scalar(ident[:, :], iota_i[:, :], 0.0,
                                         ALU.is_equal)
                eng.tensor_single_scalar(mask01[:, :], iota_i[:, :], 0.0,
                                         ALU.is_ge)
                eng.memset(ebias[:, :], EXP_SHIFT)
                eng.memset(zcol[:, :], 0.0)
                eng.memset(rcol[:, :], RC)
                eng.memset(nrcol[:, :], -RC)
                eng.memset(ones_f[:, :], 1.0)
                eng.memset(psall[:, :], 0.0).then_inc(sem_v, 1)
            emit("v", f)
            v_const = bump("v")

            W("t", "d", d_ws)
            W("t", "v", v_const)
            def f(eng):
                eng.matmul(pp[:, 0:1], ones_f[:, :], ws_sb[:, :],
                           start=True, stop=True, skip_group_check=True)
                eng.matmul(pp[:, 512:1024], ones_f[:, :], bias_sb[:, 0:512],
                           start=True, stop=True,
                           skip_group_check=True).then_inc(sem_t, 1)
            emit("t", f)
            t_bc1 = bump("t")

            W("v", "t", t_bc1)
            def f(eng):
                eng.tensor_scalar(wsb127[:, :], pp[:, 0:1], 1.0 / 127.0,
                                  None, ALU.mult)
                eng.tensor_copy(bias_b2[:, 0:512], pp[:, 512:1024])
                eng.tensor_copy(bias_b2[:, 1024:1536],
                                pp[:, 512:1024]).then_inc(sem_v, 1)
            emit("v", f)
            v_bc1 = bump("v")

            W("t", "v", v_bc1)
            def f(eng):
                eng.matmul(pp[:, 512:1024], ones_f[:, :],
                           bias_sb[:, 512:1024], start=True, stop=True,
                           skip_group_check=True).then_inc(sem_t, 1)
            emit("t", f)
            t_bc2 = bump("t")

            W("v", "t", t_bc2)
            def f(eng):
                eng.tensor_copy(bias_b2[:, 512:1024], pp[:, 512:1024])
                eng.tensor_copy(bias_b2[:, 1536:2048],
                                pp[:, 512:1024]).then_inc(sem_v, 1)
            emit("v", f)
            return bump("v")

        def ph1(b, v_ready, prev_t_end, prev_v_end, prev_s_end,
                pre=None):
            if pre is None:
                W("g", "t", prev_t_end)
                for c in range(4):
                    def f(eng, c=c, b=b):
                        src = ap(xt_h, b * D * S + (2 * c) * 128 * S,
                                 (S, 128), (128 * S, 2), (1, S))
                        dst = ap(xT, (2 * c) * 2048, (16384, 128),
                                 (2048, 2), (1, 2048))
                        eng.dma_start(dst, src).then_inc(sem_gx, 16)
                    emit("g", f)
                    bump("gx", 16)
                xt_done = ctr["gx"]

            def wload(w_handle, slot, wait_t):
                W("g", "t", wait_t)
                key = "gd" if slot == 0 else "gb"
                sem_w = sem_g if slot == 0 else sem_gb
                def f(eng, w_handle=w_handle, slot=slot, sem_w=sem_w):
                    src = ap(w_handle, 0, (D, 128), (128 * D, 8), (1, D))
                    dst = ap(wslot[slot], 0, (8192, 128), (1024, 8),
                             (1, 1024))
                    eng.dma_start(dst, src).then_inc(sem_w, 16)
                emit("g", f)
                return (key, bump(key, 16))

            if pre is None:
                wq_done = wload(wq_h, 0, prev_t_end)
                wk_done = wload(wk_h, 1, prev_t_end)
            else:
                xt_done = pre["xt"]
                wq_done = pre["wq"]
                wk_done = pre["wk"]

            s_evac = {}
            v_evac = {}
            half = 0
            q_last_mm = 0
            def pslot(i):
                return psall[:, (i % 4) * 512:((i % 4) + 1) * 512]

            for proj, (wdone, slot, dest) in enumerate(
                    ((wq_done, 0, QT), (wk_done, 1, KT))):
                W("t", wdone[0], wdone[1])
                if proj == 0:
                    W("t", "gx", xt_done)
                    W("t", "v", v_ready)
                    W("t", "v", prev_v_end)
                    W("t", "s", prev_s_end)
                for dt in range(8):
                    for sch in range(4):
                        if half >= 4:
                            W("t", "s", s_evac.get(half - 4))
                        def f(eng, dt=dt, sch=sch, half=half, slot=slot):
                            out = pslot(half)
                            for kt in range(8):
                                lhsT = wslot[slot][
                                    :, kt * 1024 + dt * 128:
                                    kt * 1024 + dt * 128 + 128]
                                rhs = xT[:, kt * 2048 + sch * 512:
                                         kt * 2048 + (sch + 1) * 512]
                                mm = eng.matmul(out, lhsT, rhs,
                                                start=(kt == 0),
                                                stop=(kt == 7),
                                                skip_group_check=True)
                                if kt == 7:
                                    mm.then_inc(sem_t, 1)
                        emit("t", f)
                        mmv = bump("t")
                        W("s", "t", mmv)
                        def f3(eng, dt=dt, sch=sch, half=half, dest=dest):
                            eng.copy(dest[:, dt * 2048 + sch * 512:
                                          dt * 2048 + (sch + 1) * 512],
                                     pslot(half)).then_inc(sem_s, 1)
                        emit("s", f3)
                        s_evac[half] = bump("s")
                        half += 1
                if proj == 0:
                    q_last_mm = ctr["t"]

            wv_done = wload(wv_h, 0, q_last_mm)

            W("v", "s", prev_s_end)
            def f(eng):
                eng.memset(ap(Vx, 64, (16896, 128), (1056, 16), (66, 16),
                              (1, 1)), 1.0).then_inc(sem_v, 1)
            emit("v", f)
            vones = bump("v")

            v_upto = {}

            def emit_vgroup(g):
                st, dc = g // 2, g % 2
                n = ppu["n"]
                ppu["n"] += 1
                phh = n % 2
                if g < 2:
                    W("t", wv_done[0], wv_done[1])
                W("t", "v", ppu["ev"].get(n - 2))
                def f(eng, st=st, dc=dc, phh=phh):
                    out = pp[:, phh * 512:(phh + 1) * 512]
                    for kt in range(8):
                        lhsT = xT[:, kt * 2048 + st * 128:
                                  kt * 2048 + st * 128 + 128]
                        rhs = wslot[0][:, kt * 1024 + dc * 512:
                                       kt * 1024 + (dc + 1) * 512]
                        mm = eng.matmul(out, lhsT, rhs, start=(kt == 0),
                                        stop=(kt == 7),
                                        skip_group_check=True)
                        if kt == 7:
                            mm.then_inc(sem_t, 1)
                emit("t", f)
                mmv = bump("t")
                W("v", "t", mmv)
                def f2(eng, st=st, dc=dc, phh=phh):
                    dst = ap(Vx, st * 1056 + dc * 8 * 66,
                             (16896, 128), (66, 8), (1, 64))
                    src = ap(psall, 3072 + phh * 512, (4096, 128),
                             (64, 8), (1, 64))
                    eng.tensor_copy(dst, src).then_inc(sem_v, 1)
                emit("v", f2)
                val = bump("v")
                ppu["ev"][n] = val
                v_upto[st] = val

            qk_last_s = s_evac[63]
            return qk_last_s, vones, emit_vgroup, v_upto

        def ph2(b, wait_s0, vones, emit_vg, v_upto):
            def wload2(w_handle, slot, wait_t):
                W("g", "t", wait_t)
                key = "gd" if slot == 0 else "gb"
                sem_w = sem_g if slot == 0 else sem_gb
                def f(eng, w_handle=w_handle, slot=slot, sem_w=sem_w):
                    src = ap(w_handle, 0, (D, 128), (128 * D, 8), (1, D))
                    dst = ap(wslot[slot], 0, (8192, 128), (1024, 8),
                             (1, 1024))
                    eng.dma_start(dst, src).then_inc(sem_w, 16)
                emit("g", f)
                return (key, bump(key, 16))
            w8_done = wload2(w8_h, 1, ctr["t"])
            wo_done = None

            expv = {}
            maskv = {}
            exp_last = {0: 0, 1: 0}
            trans_ev = {}
            u = 0
            evac_all = {}
            trans_last_t = {}
            for g in range(8):
                emit_vg(g)
            vg_next = 8
            prev_evac = max(vones, v_upto[3])

            def emit_st(qc, hp, k, p, wait_s):
                W("t", "s", wait_s)
                def f(eng, qc=qc, hp=hp, k=k, p=p):
                    j = k - 4 * qc
                    colo = max(0, j) * 128
                    last = None
                    for s_ in range(2):
                        p0 = s_ * 64
                        lhsT = KT[p0:p0 + 64, hp * 2048 + k * 128:
                                  hp * 2048 + k * 128 + 128]
                        rhs = QT[p0:p0 + 64, hp * 2048 + qc * 512 + colo:
                                 hp * 2048 + (qc + 1) * 512]
                        out = psall[:, p * 1024 + s_ * 512 + colo:
                                    p * 1024 + (s_ + 1) * 512]
                        last = eng.matmul(out, lhsT, rhs, start=True,
                                          stop=True, skip_group_check=True)
                    last.then_inc(sem_t, 1)
                emit("t", f)
                return bump("t")

            def emit_pv(qc, hp, mms, n_started, n_total):
                def f(eng, hp=hp, mms=list(mms),
                      n_started=n_started, n_total=n_total):
                    last = None
                    for idx, (k, pt, qt, s_) in enumerate(mms):
                        h = hp * 2 + s_
                        i_glob = n_started + idx
                        lhsT = PT[:, pt * 1024 + s_ * 512 + qt * 128:
                                  pt * 1024 + s_ * 512 + qt * 128 + 128]
                        rhs = Vx[:, k * 1056 + h * 66:k * 1056 + h * 66 + 65]
                        out = (oA if s_ == 0 else oB)[:, qt * 65:
                                                      qt * 65 + 65]
                        last = eng.matmul(
                            out, lhsT, rhs,
                            start=(i_glob < 2),
                            stop=(i_glob >= n_total - 2),
                            skip_group_check=True)
                    last.then_inc(sem_t, 1)
                emit("t", f)
                return bump("t")

            def emit_trans(qc, i):
                t_ = 4 * qc + i // 2
                dc = i % 2
                n = ppu["n"]
                ppu["n"] += 1
                ph = n % 2
                W("t", "v", evac_all[qc])
                W("t", "v", ppu["ev"].get(n - 2))
                ro = (qc % 2) * 4096
                qt_ = t_ - 4 * qc
                def f(eng, qt_=qt_, dc=dc, ph=ph, ro=ro):
                    for kk in range(4):
                        dt = dc * 4 + kk
                        lhsT = o_ring[:, ro + qt_ * 1024 + dt * 128:
                                      ro + qt_ * 1024 + dt * 128 + 128]
                        out = pp[:, ph * 512 + kk * 128:
                                 ph * 512 + kk * 128 + 128]
                        mm = eng.matmul(out, lhsT, ident[:, :], start=True,
                                        stop=True, skip_group_check=True)
                        if kk == 3:
                            mm.then_inc(sem_t, 1)
                emit("t", f)
                mmv = bump("t")
                trans_last_t[qc] = mmv
                W("v", "t", mmv)
                def f2(eng, t_=t_, dc=dc, ph=ph):
                    dst = ap(OT, dc * 4 * 2048 + t_ * 128,
                             (16384, 128), (2048, 4), (1, 128))
                    eng.tensor_copy(dst, pp[:, ph * 512:(ph + 1) * 512])\
                       .then_inc(sem_v, 1)
                emit("v", f2)
                ppu["ev"][n] = bump("v")
                trans_ev[t_] = ppu["ev"][n]

            for qc in range(4):
                for hp in range(8):
                    if hp == 0:
                        W("t", "v", v_upto[min(4 * qc + 3, 15)])
                    n_kt = 4 * qc + 4
                    n_total = sum(2 * (4 - max(0, kk - 4 * qc))
                                  for kk in range(n_kt))
                    n_started = 0
                    pend = []
                    u0 = u
                    for k in range(n_kt):
                        p = u % 2
                        pt = u % 4
                        wait_s = expv.get(u - 2)
                        if wait_s is None and u < 2:
                            wait_s = wait_s0
                        if u < 2:
                            W("t", "v", vones)
                        emit_st(qc, hp, k, p, wait_s)
                        stv = ctr["t"]
                        if k >= 2:
                            ku = k - 2
                            uu = u - 2
                            jd = ku - 4 * qc
                            lo = 0 if jd < 0 else jd + 1
                            mms = pend + [(ku, uu % 4, qt, s_)
                                          for qt in range(lo, 4)
                                          for s_ in range(2)]
                            pend = ([(ku, uu % 4, jd, s_)
                                     for s_ in range(2)] if jd >= 0 else [])
                            if mms:
                                W("t", "g", maskv.get(uu - 1))
                                if ku == 0:
                                    W("t", "v", prev_evac)
                                emit_pv(qc, hp, mms, n_started, n_total)
                                n_started += len(mms)
                        W("s", "t", stv)
                        def f2(eng, p=p, pt=pt,
                               colo=max(0, k - 4 * qc) * 128):
                            if colo == 0:
                                eng.activation(
                                    PT[:, pt * 1024:(pt + 1) * 1024],
                                    psall[:, p * 1024:(p + 1) * 1024],
                                    AF.Exp, bias=ebias[:, :],
                                    scale=0.125).then_inc(sem_s, 1)
                            else:
                                w_ = 512 - colo
                                src = ap(psall, p * 1024 + colo,
                                         (4096, 128), (512, 2), (1, w_))
                                dst = ap(PT, pt * 1024 + colo,
                                         (4096, 128), (512, 2), (1, w_))
                                eng.activation(
                                    dst, src, AF.Exp, bias=ebias[:, :],
                                    scale=0.125).then_inc(sem_s, 1)
                        emit("s", f2)
                        expv[u] = bump("s")
                        exp_last[p] = expv[u]
                        if k >= 4 * qc:
                            W("g", "s", expv[u])
                            def f3(eng, qc=qc, k=k, pt=pt):
                                j = k - 4 * qc
                                last = None
                                for s_ in range(2):
                                    off = (pt * 1024 + s_ * 512 + j * 128)
                                    blk = PT[:, off:off + 128]
                                    last = eng.tensor_tensor(
                                        blk, blk, mask01[:, :], ALU.mult)
                                last.then_inc(sem_m, 1)
                            emit("g", f3)
                            maskv[u] = bump("g")
                        else:
                            maskv[u] = ctr["g"]
                        u += 1
                    for ku in (n_kt - 2, n_kt - 1):
                        uu = u0 + ku
                        jd = ku - 4 * qc
                        lo = 0 if jd < 0 else jd + 1
                        mms = pend + [(ku, uu % 4, qt, s_)
                                      for qt in range(lo, 4)
                                      for s_ in range(2)]
                        pend = ([(ku, uu % 4, jd, s_)
                                 for s_ in range(2)] if jd >= 0 else [])
                        W("t", "s", expv[uu])
                        W("t", "g", maskv.get(uu - 1))
                        if mms:
                            emit_pv(qc, hp, mms, n_started, n_total)
                            n_started += len(mms)
                    if pend:
                        W("t", "g", maskv[u0 + n_kt - 1])
                        emit_pv(qc, hp, pend, n_started, n_total)
                        n_started += len(pend)
                        pend = []
                    assert n_started == n_total, (qc, hp, n_started, n_total)
                    pv_last = ctr["t"]

                    W("v", "t", pv_last)
                    if hp == 0:
                        W("v", "t", trans_last_t.get(qc - 2))
                        if qc == 0:
                            W("v", "o", ctr["o"])
                            W("v", "o2", ctr["o2"])
                    def f5(eng, qc=qc, hp=hp):
                        eng.reciprocal(rinv[:, 0:4],
                                       ap(psall, 2048 + 64, (4096, 128),
                                          (65, 4), (1, 1)))
                        eng.reciprocal(rinv[:, 4:8],
                                       ap(psall, 2560 + 64, (4096, 128),
                                          (65, 4), (1, 1)))
                        eng.drain()
                        last = None
                        ro = (qc % 2) * 4096
                        for s_ in range(2):
                            h = hp * 2 + s_
                            ps = oA if s_ == 0 else oB
                            for qt in range(4):
                                last = eng.tensor_scalar(
                                    o_ring[:, ro + qt * 1024 + h * 64:
                                           ro + qt * 1024 + h * 64 + 64],
                                    ps[:, qt * 65:qt * 65 + 64],
                                    rinv[:, s_ * 4 + qt:s_ * 4 + qt + 1],
                                    None, ALU.mult)
                        last.then_inc(sem_v, 1)
                    emit("v", f5)
                    prev_evac = bump("v")

                    if qc == 0:
                        while vg_next < min(32, 8 + 3 * (hp + 1)):
                            emit_vg(vg_next)
                            vg_next += 1
                    if qc >= 1:
                        while vg_next < 32:
                            emit_vg(vg_next)
                            vg_next += 1
                        if wo_done is None:
                            wo_done = wload2(wo_h, 0, ctr["t"])
                        emit_trans(qc - 1, hp)
                evac_all[qc] = prev_evac
            for i in range(8):
                emit_trans(3, i)
            return (ppu["ev"][ppu["n"] - 1], wo_done, w8_done,
                    exp_last, trans_ev)

        def ph45(b, trans_done, wo_done, w8_done, gpair, exp_last,
                 trans_ev, preload_cb=None):
            qte = {}
            qtmm_v = {}
            outv = gpair[2]
            o3v = {}
            outrd = {}
            x1pbv = {}
            wo_y2v = {}
            z_done = {}
            statv = {}
            qv = {}
            rsd = {}
            appd = {}
            ffnv = {}
            stats2v = {}
            appd2 = {}
            NP = 16

            def bufP(t):
                return psall[:, (t % 4) * 1024:((t % 4) + 1) * 1024]

            def zq(t):
                zt = z2 if (t % 4) < 2 else z2x
                return zt[:, (t % 2) * 1024:((t % 2) + 1) * 1024]

            def xh(t):
                return xstg[:, (t % 2) * 1024:((t % 2) + 1) * 1024]

            def x1b2h(t):
                return x1b2[:, (t % 2) * 1024:((t % 2) + 1) * 1024]

            def qih(t):
                return qint2[:, (t % 2) * 1024:((t % 2) + 1) * 1024]

            def emit_wo(t):
                if t == 0:
                    W("t", wo_done[0], wo_done[1])
                if t < 4:
                    W("t", "v", trans_ev[t])
                    if t == 0:
                        W("t", "s", exp_last[0])
                    elif t == 1:
                        W("t", "s", exp_last[1])
                    else:
                        W("t", "v", trans_done)
                else:
                    W("t", "s", qte.get(t - 4))
                    W("t", "v", o3v.get(t - 4))
                def f(eng, t=t):
                    for dc in range(2):
                        out = bufP(t)[:, dc * 512:(dc + 1) * 512]
                        for dt in range(8):
                            lhsT = OT[:, dt * 2048 + t * 128:
                                      dt * 2048 + t * 128 + 128]
                            rhs = wslot[0][:, dt * 1024 + dc * 512:
                                           dt * 1024 + (dc + 1) * 512]
                            mm = eng.matmul(out, lhsT, rhs,
                                            start=(dt == 0),
                                            stop=(dt == 7),
                                            skip_group_check=True)
                            if dc == 1 and dt == 7:
                                mm.then_inc(sem_t, 1)
                emit("t", f)
                wo_y2v[t] = bump("t")

            def emit_xdma(t):
                gp = gpair[0]
                gpair[0] += 1
                W("y", "v", gpair[1].get(gp - 2))
                key = "xd" if t % 2 == 0 else "xd2"
                sem_x = SEM[key]
                def fd(eng, t=t, b=b, sem_x=sem_x):
                    src = ap(x_h, b * S * D + t * 128 * D, (D, 128), (1, D))
                    eng.dma_start(xh(t), src).then_inc(sem_x, 16)
                emit("y", fd)
                return gp, key, bump(key, 16)

            def VA(t, gp, x_key, x_dma):
                W("v", "t", wo_y2v[t])
                W("v", x_key, x_dma)
                W("v", "s", outrd.get(t - 4))
                z = zq(t)
                c0 = 14 * (t % 4)
                g2 = 2 * (t % 4)
                sh = 12 * (t % 2)
                def f2(eng, t=t, z=z):
                    eng.scalar_tensor_tensor(z[:, :], bufP(t)[:, :], 1.0,
                                             xh(t), ALU.mult,
                                             ALU.add).then_inc(sem_v, 1)
                    eng.drain()
                emit("v", f2)
                z_done[t] = bump("v")
                gpair[1][gp] = z_done[t]
                def f3(eng, z=z, c0=c0, g2=g2, sh=sh):
                    eng.tensor_scalar(dump[:, :], z[:, :],
                                      1.0, None, ALU.mult, ALU.max,
                                      accum_out=sc[:, c0 + 4:c0 + 5])
                    eng.tensor_scalar(dumpb[:, :], z[:, :],
                                      -1.0, None, ALU.mult, ALU.max,
                                      accum_out=sc[:, c0 + 5:c0 + 6])
                    eng.bn_stats(st12[:, sh:sh + 6], z[:, 0:512])
                    eng.bn_stats(st12[:, sh + 6:sh + 12], z[:, 512:1024])
                    eng.drain()
                    eng.bn_aggr(ag4[:, g2:g2 + 2], st12[:, sh:sh + 12])
                    eng.drain()
                    mean1 = ag4[:, g2:g2 + 1]
                    eng.tensor_scalar(sc[:, c0:c0 + 1],
                                      ag4[:, g2 + 1:g2 + 2],
                                      1e-5, None, ALU.add)
                    eng.tensor_scalar(sc[:, c0 + 8:c0 + 9], mean1, -1.0,
                                      None, ALU.mult)
                    eng.tensor_tensor(sc[:, c0 + 6:c0 + 7],
                                      sc[:, c0 + 4:c0 + 5], mean1,
                                      ALU.subtract)
                    eng.tensor_tensor(sc[:, c0 + 7:c0 + 8],
                                      sc[:, c0 + 5:c0 + 6], mean1, ALU.add)
                    eng.drain()
                    eng.tensor_tensor(sc[:, c0 + 6:c0 + 7],
                                      sc[:, c0 + 6:c0 + 7],
                                      sc[:, c0 + 7:c0 + 8], ALU.max)
                    eng.drain()
                    eng.tensor_tensor(sc[:, c0 + 6:c0 + 7],
                                      sc[:, c0 + 6:c0 + 7],
                                      sc[:, c0 + 6:c0 + 7], ALU.mult)
                    eng.tensor_scalar(sc[:, c0 + 7:c0 + 8],
                                      sc[:, c0:c0 + 1], 1e-10,
                                      None, ALU.mult)
                    eng.drain()
                    eng.tensor_tensor(sc[:, c0 + 1:c0 + 2],
                                      sc[:, c0 + 6:c0 + 7],
                                      sc[:, c0 + 7:c0 + 8], ALU.max)
                    eng.drain()
                    eng.reciprocal(sc[:, c0 + 2:c0 + 4],
                                   sc[:, c0:c0 + 2]).then_inc(sem_v, 1)
                emit("v", f3)
                statv[t] = bump("v")

            def SA(t):
                W("s", "v", statv[t])
                c0 = 14 * (t % 4)
                def f4(eng, t=t, c0=c0):
                    eng.activation(sc[:, c0 + 9:c0 + 11],
                                   sc[:, c0 + 2:c0 + 4], AF.Sqrt)
                    eng.activation(sc[:, c0 + 11:c0 + 12],
                                   sc[:, c0 + 1:c0 + 2], AF.Sqrt)
                    eng.drain()
                    eng.activation(sc[:, c0 + 6:c0 + 7],
                                   sc[:, c0 + 8:c0 + 9], AF.Identity,
                                   scale=sc[:, c0 + 9:c0 + 10])
                    eng.activation(sc[:, c0 + 7:c0 + 8],
                                   sc[:, c0 + 10:c0 + 11], AF.Identity,
                                   scale=127.0)
                    eng.activation(sc[:, c0 + 13:c0 + 14],
                                   sc[:, c0 + 11:c0 + 12], AF.Identity,
                                   scale=sc[:, c0 + 9:c0 + 10])
                    eng.drain()
                    eng.activation(sc[:, c0 + 12:c0 + 13],
                                   sc[:, c0 + 8:c0 + 9], AF.Identity,
                                   scale=sc[:, c0 + 7:c0 + 8])
                    eng.activation(w2a[:, t:t + 1],
                                   sc[:, c0 + 13:c0 + 14], AF.Identity,
                                   scale=wsb127[:, :]).then_inc(sem_s, 1)
                emit("s", f4)
                rsd[t] = bump("s")

            def SB(t):
                c0 = 14 * (t % 4)
                def f6(eng, t=t, c0=c0, z=zq(t)):
                    eng.activation(
                        x1[:, t * 1024:(t + 1) * 1024], z[:, :],
                        AF.Identity, bias=sc[:, c0 + 6:c0 + 7],
                        scale=sc[:, c0 + 9:c0 + 10]).then_inc(sem_s, 1)
                emit("s", f6)
                appd[t] = bump("s")

            def VC(t):
                W("v", "s", rsd[t])
                W("v", "t", qtmm_v.get(t - 2))
                c0 = 14 * (t % 4)
                def f7(eng, t=t, c0=c0, z=zq(t)):
                    eng.tensor_scalar(x1b2h(t), z[:, :],
                                      sc[:, c0 + 7:c0 + 8],
                                      sc[:, c0 + 12:c0 + 13],
                                      ALU.mult, ALU.add)
                    eng.drain()
                    eng.tensor_scalar(x1b2h(t), x1b2h(t), RC, -RC,
                                      ALU.add, ALU.add)
                    eng.drain()
                    eng.tensor_scalar(qih(t), x1b2h(t), 127.0,
                                      -128.0, ALU.min,
                                      ALU.max).then_inc(sem_v, 1)
                emit("v", f7)
                qv[t] = bump("v")

            def G_x1pb(t):
                W("g", "s", appd[t])
                W("g", "v", qv[t])
                def fg2(eng, t=t):
                    eng.tensor_tensor(x1b2h(t),
                                      x1[:, t * 1024:(t + 1) * 1024],
                                      bias_b2[:, 0:1024],
                                      ALU.add).then_inc(sem_m, 1)
                emit("g", fg2)
                x1pbv[t] = bump("g")

            def T_qt_ffn(t):
                W("t", "v", qv[t])
                def f8(eng, t=t):
                    for dt in range(8):
                        lhsT = qih(t)[:, dt * 128:dt * 128 + 128]
                        out = bufP(t)[:, dt * 128:dt * 128 + 128]
                        mm = eng.matmul(out, lhsT, ident[:, :],
                                        start=True, stop=True,
                                        skip_group_check=True)
                        if dt == 7:
                            mm.then_inc(sem_t, 1)
                emit("t", f8)
                qtmm_v[t] = bump("t")
                W("s", "t", qtmm_v[t])
                def f9(eng, t=t):
                    last = None
                    for dc in range(2):
                        dst = ap(qT, dc * 4 * 2048 + t * 128,
                                 (16384, 128), (2048, 4), (1, 128))
                        last = eng.copy(
                            dst, bufP(t)[:, dc * 512:(dc + 1) * 512])
                    last.then_inc(sem_s, 1)
                emit("s", f9)
                qte[t] = bump("s")
                W("t", "s", qte[t])
                if t == 0:
                    W("t", w8_done[0], w8_done[1])
                def f10(eng, t=t):
                    for dc in range(2):
                        out = bufP(t)[:, dc * 512:(dc + 1) * 512]
                        for dt in range(8):
                            lhsT = qT[:, dt * 2048 + t * 128:
                                      dt * 2048 + t * 128 + 128]
                            rhs = wslot[1][:, dt * 1024 + dc * 512:
                                           dt * 1024 + (dc + 1) * 512]
                            mm = eng.matmul(out, lhsT, rhs,
                                            start=(dt == 0),
                                            stop=(dt == 7),
                                            skip_group_check=True)
                            if dc == 1 and dt == 7:
                                mm.then_inc(sem_t, 1)
                emit("t", f10)
                ffnv[t] = bump("t")

            def VD(t):
                W("v", "t", ffnv[t])
                W("v", "g", x1pbv[t])
                z = zq(t)
                sh2 = 12 * (t % 2)
                c2 = 56 + 5 * (t % 2)
                g2b = 2 * (t % 2)
                def f11(eng, t=t, z=z):
                    eng.scalar_tensor_tensor(z[:, :], bufP(t)[:, :],
                                             w2a[:, t:t + 1], x1b2h(t),
                                             ALU.mult,
                                             ALU.add).then_inc(sem_v, 1)
                    eng.drain()
                emit("v", f11)
                o3v[t] = bump("v")
                def f12(eng, z=z, sh2=sh2, c2=c2, g2b=g2b):
                    eng.bn_stats(st12b[:, sh2:sh2 + 6], z[:, 0:512])
                    eng.bn_stats(st12b[:, sh2 + 6:sh2 + 12], z[:, 512:1024])
                    eng.drain()
                    eng.bn_aggr(ag4b[:, g2b:g2b + 2],
                                st12b[:, sh2:sh2 + 12])
                    eng.drain()
                    eng.tensor_scalar(sc[:, c2:c2 + 1],
                                      ag4b[:, g2b + 1:g2b + 2],
                                      1e-5, None, ALU.add)
                    eng.tensor_scalar(sc[:, c2 + 3:c2 + 4],
                                      ag4b[:, g2b:g2b + 1], -1.0, None,
                                      ALU.mult)
                    eng.drain()
                    eng.reciprocal(sc[:, c2 + 1:c2 + 2],
                                   sc[:, c2:c2 + 1]).then_inc(sem_v, 1)
                emit("v", f12)
                stats2v[t] = bump("v")

            def SD(t, gp):
                W("s", "v", stats2v[t])
                if gp >= 2:
                    _ok, _ov = outv[gp - 2]
                    W("s", _ok, _ov)
                c2 = 56 + 5 * (t % 2)
                def f15(eng, t=t, c2=c2, z=zq(t)):
                    eng.activation(sc[:, c2 + 2:c2 + 3],
                                   sc[:, c2 + 1:c2 + 2], AF.Sqrt)
                    eng.activation(z[:, :], z[:, :], AF.Identity,
                                   bias=sc[:, c2 + 3:c2 + 4], scale=1.0)
                    eng.drain()
                    eng.activation(
                        outst2[:, (t % 2) * 1024:(t % 2) * 1024 + 1024],
                        z[:, :], AF.Identity, bias=zcol[:, :],
                        scale=sc[:, c2 + 2:c2 + 3]).then_inc(sem_s, 1)
                emit("s", f15)
                outrd[t] = bump("s")

            def G_out(t, gp):
                W("g", "s", outrd[t])
                osem_ = sem_o if (t % 2) == 0 else sem_o2
                okey = "o" if (t % 2) == 0 else "o2"
                def f16(eng, t=t, b=b, osem_=osem_):
                    dst = ap(y_h, b * S * D + t * 128 * D, (D, 128), (1, D))
                    eng.dma_start(dst,
                                  outst2[:, (t % 2) * 1024:
                                         (t % 2) * 1024 + 1024]
                                  ).then_inc(osem_, 16)
                emit("g", f16)
                bump(okey, 16)
                outv[gp] = (okey, ctr[okey])

            gps = {}
            for t in range(4):
                emit_wo(t)
            for t in range(2):
                gps[t] = emit_xdma(t)
                VA(t, *gps[t])
                SA(t)
                SB(t)
                VC(t)
                G_x1pb(t)
                T_qt_ffn(t)
            for t in (2, 3):
                gps[t] = emit_xdma(t)
                VA(t, *gps[t])
                SA(t)
            for i in range(NP):
                VD(i)
                if preload_cb is not None and i in (13, 15):
                    preload_cb(i, wo_y2v, ffnv)
                if i + 4 < NP:
                    emit_wo(i + 4)
                SD(i, gps[i][0])
                G_out(i, gps[i][0])
                if i + 2 < NP:
                    SB(i + 2)
                    VC(i + 2)
                    G_x1pb(i + 2)
                    T_qt_ffn(i + 2)
                if i + 4 < NP:
                    gps[i + 4] = emit_xdma(i + 4)
                    VA(i + 4, *gps[i + 4])
                    SA(i + 4)
            return ctr["v"]

        v_ready = ph0()
        gpair = [0, {}, {}]
        prev_t_end = 0
        prev_v_end = 0
        prev_s_end = 0
        pre = None
        for b in range(BPC):
            qk_s, vones, emit_vg, v_upto = ph1(b, v_ready, prev_t_end,
                                               prev_v_end, prev_s_end,
                                               pre)
            (trans_done, wo_done, w8_done, exp_last,
             trans_ev) = ph2(b, qk_s, vones, emit_vg, v_upto)

            nxt = {}
            def preload_cb(i, wo_y2v, ffnv, nxt=nxt, bn=b + 1):
                if i == 13:
                    W("g", "t", wo_y2v[15])
                    def fq(eng):
                        src = ap(wq_h, 0, (D, 128), (128 * D, 8), (1, D))
                        dst = ap(wslot[0], 0, (8192, 128), (1024, 8),
                                 (1, 1024))
                        eng.dma_start(dst, src).then_inc(sem_g, 16)
                    emit("g", fq)
                    nxt["wq"] = ("gd", bump("gd", 16))
                    W("g", "t", ffnv[13])
                    for c in range(4):
                        def fa(eng, c=c, bn=bn):
                            src = ap(xt_h, bn * D * S + (2 * c) * 128 * S,
                                     (S, 128), (128 * S, 2), (1, 1792))
                            dst = ap(xT, (2 * c) * 2048, (16384, 128),
                                     (2048, 2), (1, 1792))
                            eng.dma_start(dst, src).then_inc(sem_gx, 16)
                        emit("g", fa)
                        bump("gx", 16)
                else:
                    W("g", "t", ffnv[15])
                    def fk(eng):
                        src = ap(wk_h, 0, (D, 128), (128 * D, 8), (1, D))
                        dst = ap(wslot[1], 0, (8192, 128), (1024, 8),
                                 (1, 1024))
                        eng.dma_start(dst, src).then_inc(sem_gb, 16)
                    emit("g", fk)
                    nxt["wk"] = ("gb", bump("gb", 16))
                    for c in range(4):
                        def fb(eng, c=c, bn=bn):
                            src = ap(xt_h,
                                     bn * D * S + (2 * c) * 128 * S + 1792,
                                     (S, 128), (128 * S, 2), (1, 256))
                            dst = ap(xT, (2 * c) * 2048 + 1792,
                                     (16384, 128), (2048, 2), (1, 256))
                            eng.dma_start(dst, src).then_inc(sem_gx, 16)
                        emit("g", fb)
                        bump("gx", 16)
                    nxt["xt"] = ctr["gx"]

            cb = preload_cb if b + 1 < BPC else None
            ph45(b, trans_done, wo_done, w8_done, gpair, exp_last,
                 trans_ev, cb)
            pre = nxt if b + 1 < BPC else None
            prev_t_end = ctr["t"]
            prev_v_end = ctr["v"]
            prev_s_end = ctr["s"]

        to0, to1 = ctr["o"], ctr["o2"]
        def _final(eng):
            eng.wait_ge(sem_o, to0)
            eng.wait_ge(sem_o2, to1)
        emit("g", _final)

        @block.tensor
        def _(eng):
            for fn in ops["t"]:
                fn(eng)

        @block.vector
        def _(eng):
            for fn in ops["v"]:
                fn(eng)

        @block.scalar
        def _(eng):
            for fn in ops["s"]:
                fn(eng)

        @block.gpsimd
        def _(eng):
            for fn in ops["g"]:
                fn(eng)

        @block.sync
        def _(eng):
            for fn in ops["y"]:
                fn(eng)

    return nc


_NC_CACHE = None


def kernel(L0_attn_x, L0_attn_WQ, L0_attn_WK, L0_attn_WV, L0_attn_WO,
           L0_ffn_w_int8, L0_ffn_weight_scale, L0_ffn_bias):
    global _NC_CACHE
    x = np.ascontiguousarray(np.asarray(L0_attn_x, dtype=np.float32))
    xt = np.ascontiguousarray(np.swapaxes(x, 1, 2))
    wqt = np.ascontiguousarray(np.asarray(L0_attn_WQ, np.float32).T)
    wkt = np.ascontiguousarray(np.asarray(L0_attn_WK, np.float32).T)
    wvt = np.ascontiguousarray(np.asarray(L0_attn_WV, np.float32).T)
    wot = np.ascontiguousarray(np.asarray(L0_attn_WO, np.float32).T)
    w8t = np.ascontiguousarray(np.asarray(L0_ffn_w_int8, np.int32).T)
    wsc = np.asarray(L0_ffn_weight_scale, np.float32).reshape(1, 1)
    bias = np.asarray(L0_ffn_bias, np.float32).reshape(1, D)

    if _NC_CACHE is None:
        _NC_CACHE = build_nc()
    nc = _NC_CACHE

    in_maps = []
    for c in range(NCORE):
        in_maps.append({
            "x": np.ascontiguousarray(x[c * BPC:(c + 1) * BPC]),
            "xt": np.ascontiguousarray(xt[c * BPC:(c + 1) * BPC]),
            "wqt": wqt, "wkt": wkt, "wvt": wvt, "wot": wot,
            "w8t": w8t, "wsc": wsc, "bias": bias,
        })
    res = run_bass_kernel_spmd(nc, in_maps, core_ids=list(range(NCORE)))
    out = np.concatenate([np.asarray(r["y"]) for r in res.results], axis=0)
    return np.ascontiguousarray(out.astype(np.float32))


# version 38
# speedup vs baseline: 1.0422x; 1.0034x over previous
import numpy as np
from contextlib import ExitStack

import concourse.bass as bass
import concourse.mybir as mybir
from concourse.bass_utils import run_bass_kernel_spmd

F32 = mybir.dt.float32
BF16 = mybir.dt.bfloat16
I32 = mybir.dt.int32
AF = mybir.ActivationFunctionType
ALU = mybir.AluOpType
AX = mybir.AxisListType

NCORE = 8
BPC = 2
S = 2048
D = 1024
RC = 12582912.0
EXP_SHIFT = -15.0

BASE = 20480
OFF_WA = BASE + 0
OFF_WB = BASE + 16384
OFF_X = BASE + 32768
OFF_Q = BASE + 65536
OFF_K = BASE + 98304
OFF_V = BASE + 131072
OFF_Z = BASE + 131072
OFF_X1B = BASE + 139264
OFF_QI = BASE + 147456
OFF_PT = BASE + 164864
OFF_XSG = BASE + 173056
OFF_OS = BASE + 181248
OFF_BB = BASE + 197632
OFF_SM = BASE + 205824


def build_nc():
    nc = bass.Bass("TRN2", target_bir_lowering=False)

    x_h = nc.dram_tensor("x", [BPC, S, D], F32, kind="ExternalInput")
    xt_h = nc.dram_tensor("xt", [BPC, D, S], F32, kind="ExternalInput")
    wq_h = nc.dram_tensor("wqt", [D, D], F32, kind="ExternalInput")
    wk_h = nc.dram_tensor("wkt", [D, D], F32, kind="ExternalInput")
    wv_h = nc.dram_tensor("wvt", [D, D], F32, kind="ExternalInput")
    wo_h = nc.dram_tensor("wot", [D, D], F32, kind="ExternalInput")
    w8_h = nc.dram_tensor("w8t", [D, D], I32, kind="ExternalInput")
    ws_h = nc.dram_tensor("wsc", [1, 1], F32, kind="ExternalInput")
    bias_h = nc.dram_tensor("bias", [1, D], F32, kind="ExternalInput")
    y_h = nc.dram_tensor("y", [BPC, S, D], F32, kind="ExternalOutput")

    with ExitStack() as ctx:
        e = ctx.enter_context

        def at(name, shape, dtype, off):
            return nc.alloc_sbuf_tensor_at(name, shape, dtype, offset=off)

        wslot = [at("wslotA", [128, 8192], BF16, OFF_WA),
                 at("wslotB", [128, 8192], BF16, OFF_WB)]
        xT = at("xT", [128, 16384], BF16, OFF_X)
        qT = at("qT", [128, 16384], BF16, OFF_X)
        o_ring = at("o_ring", [128, 8192], BF16, OFF_OS)
        QT = at("QT", [128, 16384], BF16, OFF_Q)
        OT = at("OT", [128, 16384], BF16, OFF_Q)
        KT = at("KT", [128, 16384], BF16, OFF_K)
        x1 = at("x1", [128, 16384], BF16, OFF_K)
        Vx = at("Vx", [128, 16896], BF16, OFF_V)
        z2 = at("z2", [128, 2048], F32, OFF_Z)
        z2x = at("z2x", [128, 2048], F32, OFF_V + 20480)
        dump = at("dump", [128, 1024], BF16, OFF_V + 28672)
        dumpb = at("dumpb", [128, 1024], BF16, OFF_V + 30720)
        x1b2 = at("x1b2", [128, 2048], F32, OFF_X1B)
        qint2 = at("qint2", [128, 2048], BF16, OFF_QI)
        PT = at("PT", [128, 4096], BF16, OFF_PT)
        xstg = at("xstg", [128, 2048], F32, OFF_XSG)
        outst2 = at("outst2", [128, 4096], F32, OFF_OS)
        bias_b2 = at("bias_b2", [128, 2048], F32, OFF_BB)

        o = OFF_SM
        def small(name, shape, dtype):
            nonlocal o
            t = at(name, shape, dtype, o)
            nbytes = shape[1] * {F32: 4, BF16: 2, I32: 4}[dtype]
            o += (nbytes + 31) // 32 * 32
            return t
        ident = small("ident", [128, 128], BF16)
        mask01 = small("mask01", [128, 128], BF16)
        iota_i = small("iota_i", [128, 128], I32)
        ones_f = small("ones_f", [1, 128], F32)
        ws_sb = small("ws_sb", [1, 1], F32)
        bias_sb = at("bias_sb", [1, 1024], F32, OFF_OS)
        ebias = small("ebias", [128, 1], F32)
        zcol = small("zcol", [128, 1], F32)
        rcol = small("rcol", [128, 1], F32)
        nrcol = small("nrcol", [128, 1], F32)
        wsb127 = small("wsb127", [128, 1], F32)
        st12 = small("st12", [128, 24], F32)
        ag4 = small("ag4", [128, 8], F32)
        sc = small("sc", [128, 68], F32)
        rinv = small("rinv", [128, 8], F32)
        w2a = small("w2a", [128, 16], F32)
        st12b = small("st12b", [128, 24], F32)
        ag4b = small("ag4b", [128, 4], F32)
        assert o <= 229000, o

        psall = e(nc.psum_tensor("psall", [128, 4096], F32))
        stbuf = [lambda p=p: psall[:, p * 1024:(p + 1) * 1024]
                 for p in range(2)]
        oA = psall[:, 2048:2308]
        oB = psall[:, 2560:2820]
        pp = psall[:, 3072:4096]
        bufY = [psall[:, 0:2048], psall[:, 2048:4096]]

        sem_d = e(nc.semaphore("sem_d"))
        sem_g = e(nc.semaphore("sem_g"))
        sem_o = e(nc.semaphore("sem_o"))
        sem_t = e(nc.semaphore("sem_t"))
        sem_s = e(nc.semaphore("sem_s"))
        sem_v = e(nc.semaphore("sem_v"))
        sem_m = e(nc.semaphore("sem_m"))
        sem_gx = e(nc.semaphore("sem_gx"))
        sem_gb = e(nc.semaphore("sem_gb"))
        sem_o2 = e(nc.semaphore("sem_o2"))
        sem_xd = e(nc.semaphore("sem_xd"))
        sem_xd2 = e(nc.semaphore("sem_xd2"))
        block = e(nc.Block())

        ops = {k: [] for k in ("t", "v", "s", "g", "y")}
        ppu = {"n": 0, "ev": {}}
        ctr = {k: 0 for k in ("t", "v", "s", "g", "d", "gd", "o",
                              "gx", "gb", "o2", "xd", "xd2")}
        SEM = {"t": sem_t, "v": sem_v, "s": sem_s, "g": sem_m,
               "d": sem_d, "gd": sem_g, "o": sem_o, "gx": sem_gx,
               "gb": sem_gb, "o2": sem_o2, "xd": sem_xd, "xd2": sem_xd2}

        def emit(engine, fn):
            ops[engine].append(fn)

        def bump(key, n=1):
            ctr[key] += n
            return ctr[key]

        def W(engine, key, val):
            if val is None or val <= 0:
                return
            sem = SEM[key]
            emit(engine, lambda eng, sem=sem, val=val: eng.wait_ge(sem, val))

        def ap(t, off, *dims):
            return bass.AP(t, off, [list(d) for d in dims])

        def ph0():
            def f(eng):
                eng.dma_start(ws_sb[:, :], ws_h.ap()).then_inc(sem_d, 16)
                eng.dma_start(bias_sb[:, :], bias_h.ap()).then_inc(sem_d, 16)
            emit("y", f)
            d_ws = bump("d", 32)

            def f(eng):
                eng.iota(iota_i[:, :], [[1, 128]],
                         channel_multiplier=-1).then_inc(sem_m, 1)
            emit("g", f)
            g_iota = bump("g")

            W("v", "g", g_iota)
            def f(eng):
                eng.tensor_single_scalar(ident[:, :], iota_i[:, :], 0.0,
                                         ALU.is_equal)
                eng.tensor_single_scalar(mask01[:, :], iota_i[:, :], 0.0,
                                         ALU.is_ge)
                eng.memset(ebias[:, :], EXP_SHIFT)
                eng.memset(zcol[:, :], 0.0)
                eng.memset(rcol[:, :], RC)
                eng.memset(nrcol[:, :], -RC)
                eng.memset(ones_f[:, :], 1.0)
                eng.memset(psall[:, :], 0.0).then_inc(sem_v, 1)
            emit("v", f)
            v_const = bump("v")

            W("t", "d", d_ws)
            W("t", "v", v_const)
            def f(eng):
                eng.matmul(pp[:, 0:1], ones_f[:, :], ws_sb[:, :],
                           start=True, stop=True, skip_group_check=True)
                eng.matmul(pp[:, 512:1024], ones_f[:, :], bias_sb[:, 0:512],
                           start=True, stop=True,
                           skip_group_check=True).then_inc(sem_t, 1)
            emit("t", f)
            t_bc1 = bump("t")

            W("v", "t", t_bc1)
            def f(eng):
                eng.tensor_scalar(wsb127[:, :], pp[:, 0:1], 1.0 / 127.0,
                                  None, ALU.mult)
                eng.tensor_copy(bias_b2[:, 0:512], pp[:, 512:1024])
                eng.tensor_copy(bias_b2[:, 1024:1536],
                                pp[:, 512:1024]).then_inc(sem_v, 1)
            emit("v", f)
            v_bc1 = bump("v")

            W("t", "v", v_bc1)
            def f(eng):
                eng.matmul(pp[:, 512:1024], ones_f[:, :],
                           bias_sb[:, 512:1024], start=True, stop=True,
                           skip_group_check=True).then_inc(sem_t, 1)
            emit("t", f)
            t_bc2 = bump("t")

            W("v", "t", t_bc2)
            def f(eng):
                eng.tensor_copy(bias_b2[:, 512:1024], pp[:, 512:1024])
                eng.tensor_copy(bias_b2[:, 1536:2048],
                                pp[:, 512:1024]).then_inc(sem_v, 1)
            emit("v", f)
            return bump("v")

        def ph1(b, v_ready, prev_t_end, prev_v_end, prev_s_end,
                pre=None):
            if pre is None:
                W("g", "t", prev_t_end)
                for c in range(4):
                    def f(eng, c=c, b=b):
                        src = ap(xt_h, b * D * S + (2 * c) * 128 * S,
                                 (S, 128), (128 * S, 2), (1, S))
                        dst = ap(xT, (2 * c) * 2048, (16384, 128),
                                 (2048, 2), (1, 2048))
                        eng.dma_start(dst, src).then_inc(sem_gx, 16)
                    emit("g", f)
                    bump("gx", 16)
                xt_done = ctr["gx"]

            def wload(w_handle, slot, wait_t):
                W("g", "t", wait_t)
                key = "gd" if slot == 0 else "gb"
                sem_w = sem_g if slot == 0 else sem_gb
                def f(eng, w_handle=w_handle, slot=slot, sem_w=sem_w):
                    src = ap(w_handle, 0, (D, 128), (128 * D, 8), (1, D))
                    dst = ap(wslot[slot], 0, (8192, 128), (1024, 8),
                             (1, 1024))
                    eng.dma_start(dst, src).then_inc(sem_w, 16)
                emit("g", f)
                return (key, bump(key, 16))

            if pre is None:
                wq_done = wload(wq_h, 0, prev_t_end)
                wk_done = wload(wk_h, 1, prev_t_end)
            else:
                xt_done = pre["xt"]
                wq_done = pre["wq"]
                wk_done = pre["wk"]

            s_evac = {}
            v_evac = {}
            half = 0
            q_last_mm = 0
            def pslot(i):
                return psall[:, (i % 4) * 512:((i % 4) + 1) * 512]

            for proj, (wdone, slot, dest) in enumerate(
                    ((wq_done, 0, QT), (wk_done, 1, KT))):
                W("t", wdone[0], wdone[1])
                if proj == 0:
                    W("t", "gx", xt_done)
                    W("t", "v", v_ready)
                    W("t", "v", prev_v_end)
                    W("t", "s", prev_s_end)
                for dt in range(8):
                    for sch in range(4):
                        if half >= 4:
                            W("t", "s", s_evac.get(half - 4))
                        def f(eng, dt=dt, sch=sch, half=half, slot=slot):
                            out = pslot(half)
                            for kt in range(8):
                                lhsT = wslot[slot][
                                    :, kt * 1024 + dt * 128:
                                    kt * 1024 + dt * 128 + 128]
                                rhs = xT[:, kt * 2048 + sch * 512:
                                         kt * 2048 + (sch + 1) * 512]
                                mm = eng.matmul(out, lhsT, rhs,
                                                start=(kt == 0),
                                                stop=(kt == 7),
                                                skip_group_check=True)
                                if kt == 7:
                                    mm.then_inc(sem_t, 1)
                        emit("t", f)
                        mmv = bump("t")
                        W("s", "t", mmv)
                        def f3(eng, dt=dt, sch=sch, half=half, dest=dest):
                            eng.copy(dest[:, dt * 2048 + sch * 512:
                                          dt * 2048 + (sch + 1) * 512],
                                     pslot(half)).then_inc(sem_s, 1)
                        emit("s", f3)
                        s_evac[half] = bump("s")
                        half += 1
                if proj == 0:
                    q_last_mm = ctr["t"]

            wv_done = wload(wv_h, 0, q_last_mm)

            W("v", "s", prev_s_end)
            def f(eng):
                eng.memset(ap(Vx, 64, (16896, 128), (1056, 16), (66, 16),
                              (1, 1)), 1.0).then_inc(sem_v, 1)
            emit("v", f)
            vones = bump("v")

            v_upto = {}

            def emit_vgroup(g):
                st, dc = g // 2, g % 2
                n = ppu["n"]
                ppu["n"] += 1
                phh = n % 2
                if g < 2:
                    W("t", wv_done[0], wv_done[1])
                W("t", "v", ppu["ev"].get(n - 2))
                def f(eng, st=st, dc=dc, phh=phh):
                    out = pp[:, phh * 512:(phh + 1) * 512]
                    for kt in range(8):
                        lhsT = xT[:, kt * 2048 + st * 128:
                                  kt * 2048 + st * 128 + 128]
                        rhs = wslot[0][:, kt * 1024 + dc * 512:
                                       kt * 1024 + (dc + 1) * 512]
                        mm = eng.matmul(out, lhsT, rhs, start=(kt == 0),
                                        stop=(kt == 7),
                                        skip_group_check=True)
                        if kt == 7:
                            mm.then_inc(sem_t, 1)
                emit("t", f)
                mmv = bump("t")
                W("v", "t", mmv)
                def f2(eng, st=st, dc=dc, phh=phh):
                    dst = ap(Vx, st * 1056 + dc * 8 * 66,
                             (16896, 128), (66, 8), (1, 64))
                    src = ap(psall, 3072 + phh * 512, (4096, 128),
                             (64, 8), (1, 64))
                    eng.tensor_copy(dst, src).then_inc(sem_v, 1)
                emit("v", f2)
                val = bump("v")
                ppu["ev"][n] = val
                v_upto[st] = val

            qk_last_s = s_evac[63]
            return qk_last_s, vones, emit_vgroup, v_upto

        def ph2(b, wait_s0, vones, emit_vg, v_upto):
            def wload2(w_handle, slot, wait_t):
                W("g", "t", wait_t)
                key = "gd" if slot == 0 else "gb"
                sem_w = sem_g if slot == 0 else sem_gb
                def f(eng, w_handle=w_handle, slot=slot, sem_w=sem_w):
                    src = ap(w_handle, 0, (D, 128), (128 * D, 8), (1, D))
                    dst = ap(wslot[slot], 0, (8192, 128), (1024, 8),
                             (1, 1024))
                    eng.dma_start(dst, src).then_inc(sem_w, 16)
                emit("g", f)
                return (key, bump(key, 16))
            w8_done = wload2(w8_h, 1, ctr["t"])
            wo_done = None

            expv = {}
            maskv = {}
            exp_last = {0: 0, 1: 0}
            trans_ev = {}
            u = 0
            evac_all = {}
            trans_last_t = {}
            for g in range(8):
                emit_vg(g)
            vg_next = 8
            prev_evac = max(vones, v_upto[3])

            def emit_st(qc, hp, k, p, wait_s):
                W("t", "s", wait_s)
                def f(eng, qc=qc, hp=hp, k=k, p=p):
                    j = k - 4 * qc
                    colo = max(0, j) * 128
                    last = None
                    for s_ in range(2):
                        p0 = s_ * 64
                        lhsT = KT[p0:p0 + 64, hp * 2048 + k * 128:
                                  hp * 2048 + k * 128 + 128]
                        rhs = QT[p0:p0 + 64, hp * 2048 + qc * 512 + colo:
                                 hp * 2048 + (qc + 1) * 512]
                        out = psall[:, p * 1024 + s_ * 512 + colo:
                                    p * 1024 + (s_ + 1) * 512]
                        last = eng.matmul(out, lhsT, rhs, start=True,
                                          stop=True, skip_group_check=True)
                    last.then_inc(sem_t, 1)
                emit("t", f)
                return bump("t")

            def emit_pv(qc, hp, mms, n_started, n_total):
                def f(eng, hp=hp, mms=list(mms),
                      n_started=n_started, n_total=n_total):
                    last = None
                    for idx, (k, pt, qt, s_) in enumerate(mms):
                        h = hp * 2 + s_
                        i_glob = n_started + idx
                        lhsT = PT[:, pt * 1024 + s_ * 512 + qt * 128:
                                  pt * 1024 + s_ * 512 + qt * 128 + 128]
                        rhs = Vx[:, k * 1056 + h * 66:k * 1056 + h * 66 + 65]
                        out = (oA if s_ == 0 else oB)[:, qt * 65:
                                                      qt * 65 + 65]
                        last = eng.matmul(
                            out, lhsT, rhs,
                            start=(i_glob < 2),
                            stop=(i_glob >= n_total - 2),
                            skip_group_check=True)
                    last.then_inc(sem_t, 1)
                emit("t", f)
                return bump("t")

            def emit_trans(qc, i):
                t_ = 4 * qc + i // 2
                dc = i % 2
                n = ppu["n"]
                ppu["n"] += 1
                ph = n % 2
                W("t", "v", evac_all[qc])
                W("t", "v", ppu["ev"].get(n - 2))
                ro = (qc % 2) * 4096
                qt_ = t_ - 4 * qc
                def f(eng, qt_=qt_, dc=dc, ph=ph, ro=ro):
                    for kk in range(4):
                        dt = dc * 4 + kk
                        lhsT = o_ring[:, ro + qt_ * 1024 + dt * 128:
                                      ro + qt_ * 1024 + dt * 128 + 128]
                        out = pp[:, ph * 512 + kk * 128:
                                 ph * 512 + kk * 128 + 128]
                        mm = eng.matmul(out, lhsT, ident[:, :], start=True,
                                        stop=True, skip_group_check=True)
                        if kk == 3:
                            mm.then_inc(sem_t, 1)
                emit("t", f)
                mmv = bump("t")
                trans_last_t[qc] = mmv
                W("v", "t", mmv)
                def f2(eng, t_=t_, dc=dc, ph=ph):
                    dst = ap(OT, dc * 4 * 2048 + t_ * 128,
                             (16384, 128), (2048, 4), (1, 128))
                    eng.tensor_copy(dst, pp[:, ph * 512:(ph + 1) * 512])\
                       .then_inc(sem_v, 1)
                emit("v", f2)
                ppu["ev"][n] = bump("v")
                trans_ev[t_] = ppu["ev"][n]

            for qc in range(4):
                for hp in range(8):
                    if hp == 0:
                        W("t", "v", v_upto[min(4 * qc + 3, 15)])
                    n_kt = 4 * qc + 4
                    n_total = sum(2 * (4 - max(0, kk - 4 * qc))
                                  for kk in range(n_kt))
                    n_started = 0
                    pend = []
                    u0 = u
                    for k in range(n_kt):
                        p = u % 2
                        pt = u % 4
                        wait_s = expv.get(u - 2)
                        if wait_s is None and u < 2:
                            wait_s = wait_s0
                        if u < 2:
                            W("t", "v", vones)
                        emit_st(qc, hp, k, p, wait_s)
                        stv = ctr["t"]
                        if k >= 2:
                            ku = k - 2
                            uu = u - 2
                            jd = ku - 4 * qc
                            lo = 0 if jd < 0 else jd + 1
                            mms = pend + [(ku, uu % 4, qt, s_)
                                          for qt in range(lo, 4)
                                          for s_ in range(2)]
                            pend = ([(ku, uu % 4, jd, s_)
                                     for s_ in range(2)] if jd >= 0 else [])
                            if mms:
                                W("t", "g", maskv.get(uu - 1))
                                if ku == 0:
                                    W("t", "v", prev_evac)
                                emit_pv(qc, hp, mms, n_started, n_total)
                                n_started += len(mms)
                        W("s", "t", stv)
                        def f2(eng, p=p, pt=pt,
                               colo=max(0, k - 4 * qc) * 128):
                            if colo == 0:
                                eng.activation(
                                    PT[:, pt * 1024:(pt + 1) * 1024],
                                    psall[:, p * 1024:(p + 1) * 1024],
                                    AF.Exp, bias=ebias[:, :],
                                    scale=0.125).then_inc(sem_s, 1)
                            else:
                                w_ = 512 - colo
                                src = ap(psall, p * 1024 + colo,
                                         (4096, 128), (512, 2), (1, w_))
                                dst = ap(PT, pt * 1024 + colo,
                                         (4096, 128), (512, 2), (1, w_))
                                eng.activation(
                                    dst, src, AF.Exp, bias=ebias[:, :],
                                    scale=0.125).then_inc(sem_s, 1)
                        emit("s", f2)
                        expv[u] = bump("s")
                        exp_last[p] = expv[u]
                        if k >= 4 * qc:
                            W("g", "s", expv[u])
                            def f3(eng, qc=qc, k=k, pt=pt):
                                j = k - 4 * qc
                                last = None
                                for s_ in range(2):
                                    off = (pt * 1024 + s_ * 512 + j * 128)
                                    blk = PT[:, off:off + 128]
                                    last = eng.tensor_tensor(
                                        blk, blk, mask01[:, :], ALU.mult)
                                last.then_inc(sem_m, 1)
                            emit("g", f3)
                            maskv[u] = bump("g")
                        else:
                            maskv[u] = ctr["g"]
                        u += 1
                    for ku in (n_kt - 2, n_kt - 1):
                        uu = u0 + ku
                        jd = ku - 4 * qc
                        lo = 0 if jd < 0 else jd + 1
                        mms = pend + [(ku, uu % 4, qt, s_)
                                      for qt in range(lo, 4)
                                      for s_ in range(2)]
                        pend = ([(ku, uu % 4, jd, s_)
                                 for s_ in range(2)] if jd >= 0 else [])
                        W("t", "s", expv[uu])
                        W("t", "g", maskv.get(uu - 1))
                        if mms:
                            emit_pv(qc, hp, mms, n_started, n_total)
                            n_started += len(mms)
                    if pend:
                        W("t", "g", maskv[u0 + n_kt - 1])
                        emit_pv(qc, hp, pend, n_started, n_total)
                        n_started += len(pend)
                        pend = []
                    assert n_started == n_total, (qc, hp, n_started, n_total)
                    pv_last = ctr["t"]

                    W("v", "t", pv_last)
                    if hp == 0:
                        W("v", "t", trans_last_t.get(qc - 2))
                        if qc == 0:
                            W("v", "o", ctr["o"])
                            W("v", "o2", ctr["o2"])
                    def f5(eng, qc=qc, hp=hp):
                        eng.reciprocal(rinv[:, 0:4],
                                       ap(psall, 2048 + 64, (4096, 128),
                                          (65, 4), (1, 1)))
                        eng.reciprocal(rinv[:, 4:8],
                                       ap(psall, 2560 + 64, (4096, 128),
                                          (65, 4), (1, 1)))
                        eng.drain()
                        last = None
                        ro = (qc % 2) * 4096
                        for s_ in range(2):
                            h = hp * 2 + s_
                            ps = oA if s_ == 0 else oB
                            for qt in range(4):
                                last = eng.tensor_scalar(
                                    o_ring[:, ro + qt * 1024 + h * 64:
                                           ro + qt * 1024 + h * 64 + 64],
                                    ps[:, qt * 65:qt * 65 + 64],
                                    rinv[:, s_ * 4 + qt:s_ * 4 + qt + 1],
                                    None, ALU.mult)
                        last.then_inc(sem_v, 1)
                    emit("v", f5)
                    prev_evac = bump("v")

                    if qc == 0:
                        while vg_next < min(32, 8 + 3 * (hp + 1)):
                            emit_vg(vg_next)
                            vg_next += 1
                    if qc >= 1:
                        while vg_next < 32:
                            emit_vg(vg_next)
                            vg_next += 1
                        if wo_done is None:
                            wo_done = wload2(wo_h, 0, ctr["t"])
                        emit_trans(qc - 1, hp)
                evac_all[qc] = prev_evac
            for i in range(8):
                emit_trans(3, i)
            return (ppu["ev"][ppu["n"] - 1], wo_done, w8_done,
                    exp_last, trans_ev)

        def ph45(b, trans_done, wo_done, w8_done, gpair, exp_last,
                 trans_ev, preload_cb=None):
            qte = {}
            qtmm_v = {}
            outv = gpair[2]
            o3v = {}
            outrd = {}
            x1pbv = {}
            wo_y2v = {}
            z_done = {}
            statv = {}
            qv = {}
            rsd = {}
            appd = {}
            ffnv = {}
            stats2v = {}
            appd2 = {}
            NP = 16

            def bufP(t):
                return psall[:, (t % 4) * 1024:((t % 4) + 1) * 1024]

            def zq(t):
                zt = z2 if (t % 4) < 2 else z2x
                return zt[:, (t % 2) * 1024:((t % 2) + 1) * 1024]

            def xh(t):
                return xstg[:, (t % 2) * 1024:((t % 2) + 1) * 1024]

            def x1b2h(t):
                return x1b2[:, (t % 2) * 1024:((t % 2) + 1) * 1024]

            def qih(t):
                return qint2[:, (t % 2) * 1024:((t % 2) + 1) * 1024]

            def emit_wo(t):
                if t == 0:
                    W("t", wo_done[0], wo_done[1])
                if t < 4:
                    W("t", "v", trans_ev[t])
                    if t == 0:
                        W("t", "s", exp_last[0])
                    elif t == 1:
                        W("t", "s", exp_last[1])
                    else:
                        W("t", "v", trans_done)
                else:
                    W("t", "s", qte.get(t - 4))
                    W("t", "v", o3v.get(t - 4))
                def f(eng, t=t):
                    for dc in range(2):
                        out = bufP(t)[:, dc * 512:(dc + 1) * 512]
                        for dt in range(8):
                            lhsT = OT[:, dt * 2048 + t * 128:
                                      dt * 2048 + t * 128 + 128]
                            rhs = wslot[0][:, dt * 1024 + dc * 512:
                                           dt * 1024 + (dc + 1) * 512]
                            mm = eng.matmul(out, lhsT, rhs,
                                            start=(dt == 0),
                                            stop=(dt == 7),
                                            skip_group_check=True)
                            if dc == 1 and dt == 7:
                                mm.then_inc(sem_t, 1)
                emit("t", f)
                wo_y2v[t] = bump("t")

            def emit_xdma(t):
                gp = gpair[0]
                gpair[0] += 1
                W("y", "v", gpair[1].get(gp - 2))
                key = "xd" if t % 2 == 0 else "xd2"
                sem_x = SEM[key]
                def fd(eng, t=t, b=b, sem_x=sem_x):
                    src = ap(x_h, b * S * D + t * 128 * D, (D, 128), (1, D))
                    eng.dma_start(xh(t), src).then_inc(sem_x, 16)
                emit("y", fd)
                return gp, key, bump(key, 16)

            def VA(t, gp, x_key, x_dma):
                W("v", "t", wo_y2v[t])
                W("v", x_key, x_dma)
                W("v", "s", outrd.get(t - 4))
                z = zq(t)
                c0 = 14 * (t % 4)
                g2 = 2 * (t % 4)
                sh = 12 * (t % 2)
                def f2(eng, t=t, z=z):
                    eng.scalar_tensor_tensor(z[:, :], bufP(t)[:, :], 1.0,
                                             xh(t), ALU.mult,
                                             ALU.add).then_inc(sem_v, 1)
                    eng.drain()
                emit("v", f2)
                z_done[t] = bump("v")
                gpair[1][gp] = z_done[t]
                def f3(eng, z=z, c0=c0, g2=g2, sh=sh):
                    eng.tensor_scalar(dump[:, :], z[:, :],
                                      1.0, None, ALU.mult, ALU.max,
                                      accum_out=sc[:, c0 + 4:c0 + 5])
                    eng.tensor_scalar(dumpb[:, :], z[:, :],
                                      -1.0, None, ALU.mult, ALU.max,
                                      accum_out=sc[:, c0 + 5:c0 + 6])
                    eng.bn_stats(st12[:, sh:sh + 6], z[:, 0:512])
                    eng.bn_stats(st12[:, sh + 6:sh + 12], z[:, 512:1024])
                    eng.drain()
                    eng.bn_aggr(ag4[:, g2:g2 + 2], st12[:, sh:sh + 12])
                    eng.drain()
                    mean1 = ag4[:, g2:g2 + 1]
                    eng.tensor_scalar(sc[:, c0:c0 + 1],
                                      ag4[:, g2 + 1:g2 + 2],
                                      1e-5, None, ALU.add)
                    eng.tensor_scalar(sc[:, c0 + 8:c0 + 9], mean1, -1.0,
                                      None, ALU.mult)
                    eng.tensor_tensor(sc[:, c0 + 6:c0 + 7],
                                      sc[:, c0 + 4:c0 + 5], mean1,
                                      ALU.subtract)
                    eng.tensor_tensor(sc[:, c0 + 7:c0 + 8],
                                      sc[:, c0 + 5:c0 + 6], mean1, ALU.add)
                    eng.drain()
                    eng.tensor_tensor(sc[:, c0 + 6:c0 + 7],
                                      sc[:, c0 + 6:c0 + 7],
                                      sc[:, c0 + 7:c0 + 8], ALU.max)
                    eng.drain()
                    eng.tensor_tensor(sc[:, c0 + 6:c0 + 7],
                                      sc[:, c0 + 6:c0 + 7],
                                      sc[:, c0 + 6:c0 + 7], ALU.mult)
                    eng.tensor_scalar(sc[:, c0 + 7:c0 + 8],
                                      sc[:, c0:c0 + 1], 1e-10,
                                      None, ALU.mult)
                    eng.drain()
                    eng.tensor_tensor(sc[:, c0 + 1:c0 + 2],
                                      sc[:, c0 + 6:c0 + 7],
                                      sc[:, c0 + 7:c0 + 8], ALU.max)
                    eng.drain()
                    eng.reciprocal(sc[:, c0 + 2:c0 + 4],
                                   sc[:, c0:c0 + 2]).then_inc(sem_v, 1)
                emit("v", f3)
                statv[t] = bump("v")

            def SA(t):
                W("s", "v", statv[t])
                c0 = 14 * (t % 4)
                def f4(eng, t=t, c0=c0):
                    eng.activation(sc[:, c0 + 9:c0 + 11],
                                   sc[:, c0 + 2:c0 + 4], AF.Sqrt)
                    eng.activation(sc[:, c0 + 11:c0 + 12],
                                   sc[:, c0 + 1:c0 + 2], AF.Sqrt)
                    eng.drain()
                    eng.activation(sc[:, c0 + 6:c0 + 7],
                                   sc[:, c0 + 8:c0 + 9], AF.Identity,
                                   scale=sc[:, c0 + 9:c0 + 10])
                    eng.activation(sc[:, c0 + 7:c0 + 8],
                                   sc[:, c0 + 10:c0 + 11], AF.Identity,
                                   scale=127.0)
                    eng.activation(sc[:, c0 + 13:c0 + 14],
                                   sc[:, c0 + 11:c0 + 12], AF.Identity,
                                   scale=sc[:, c0 + 9:c0 + 10])
                    eng.drain()
                    eng.activation(sc[:, c0 + 12:c0 + 13],
                                   sc[:, c0 + 8:c0 + 9], AF.Identity,
                                   scale=sc[:, c0 + 7:c0 + 8])
                    eng.activation(w2a[:, t:t + 1],
                                   sc[:, c0 + 13:c0 + 14], AF.Identity,
                                   scale=wsb127[:, :]).then_inc(sem_s, 1)
                emit("s", f4)
                rsd[t] = bump("s")

            def SB(t):
                c0 = 14 * (t % 4)
                def f6(eng, t=t, c0=c0, z=zq(t)):
                    eng.activation(
                        x1[:, t * 1024:(t + 1) * 1024], z[:, :],
                        AF.Identity, bias=sc[:, c0 + 6:c0 + 7],
                        scale=sc[:, c0 + 9:c0 + 10]).then_inc(sem_s, 1)
                emit("s", f6)
                appd[t] = bump("s")

            def VC(t):
                W("v", "s", rsd[t])
                W("v", "t", qtmm_v.get(t - 2))
                c0 = 14 * (t % 4)
                def f7(eng, t=t, c0=c0, z=zq(t)):
                    eng.tensor_scalar(x1b2h(t), z[:, :],
                                      sc[:, c0 + 7:c0 + 8],
                                      sc[:, c0 + 12:c0 + 13],
                                      ALU.mult, ALU.add)
                    eng.drain()
                    eng.tensor_scalar(x1b2h(t), x1b2h(t), RC, -RC,
                                      ALU.add, ALU.add)
                    eng.drain()
                    eng.tensor_scalar(qih(t), x1b2h(t), 127.0,
                                      -128.0, ALU.min,
                                      ALU.max).then_inc(sem_v, 1)
                emit("v", f7)
                qv[t] = bump("v")

            def G_x1pb(t):
                W("g", "s", appd[t])
                W("g", "v", qv[t])
                def fg2(eng, t=t):
                    eng.tensor_tensor(x1b2h(t),
                                      x1[:, t * 1024:(t + 1) * 1024],
                                      bias_b2[:, 0:1024],
                                      ALU.add).then_inc(sem_m, 1)
                emit("g", fg2)
                x1pbv[t] = bump("g")

            def T_qt_ffn(t):
                W("t", "v", qv[t])
                def f8(eng, t=t):
                    for dt in range(8):
                        lhsT = qih(t)[:, dt * 128:dt * 128 + 128]
                        out = bufP(t)[:, dt * 128:dt * 128 + 128]
                        mm = eng.matmul(out, lhsT, ident[:, :],
                                        start=True, stop=True,
                                        skip_group_check=True)
                        if dt == 7:
                            mm.then_inc(sem_t, 1)
                emit("t", f8)
                qtmm_v[t] = bump("t")
                W("s", "t", qtmm_v[t])
                def f9(eng, t=t):
                    last = None
                    for dc in range(2):
                        dst = ap(qT, dc * 4 * 2048 + t * 128,
                                 (16384, 128), (2048, 4), (1, 128))
                        last = eng.copy(
                            dst, bufP(t)[:, dc * 512:(dc + 1) * 512])
                    last.then_inc(sem_s, 1)
                emit("s", f9)
                qte[t] = bump("s")
                W("t", "s", qte[t])
                if t == 0:
                    W("t", w8_done[0], w8_done[1])
                def f10(eng, t=t):
                    for dc in range(2):
                        out = bufP(t)[:, dc * 512:(dc + 1) * 512]
                        for dt in range(8):
                            lhsT = qT[:, dt * 2048 + t * 128:
                                      dt * 2048 + t * 128 + 128]
                            rhs = wslot[1][:, dt * 1024 + dc * 512:
                                           dt * 1024 + (dc + 1) * 512]
                            mm = eng.matmul(out, lhsT, rhs,
                                            start=(dt == 0),
                                            stop=(dt == 7),
                                            skip_group_check=True)
                            if dc == 1 and dt == 7:
                                mm.then_inc(sem_t, 1)
                emit("t", f10)
                ffnv[t] = bump("t")

            def VD(t):
                W("v", "t", ffnv[t])
                W("v", "g", x1pbv[t])
                z = zq(t)
                sh2 = 12 * (t % 2)
                c2 = 56 + 5 * (t % 2)
                g2b = 2 * (t % 2)
                def f11(eng, t=t, z=z):
                    eng.scalar_tensor_tensor(z[:, :], bufP(t)[:, :],
                                             w2a[:, t:t + 1], x1b2h(t),
                                             ALU.mult,
                                             ALU.add).then_inc(sem_v, 1)
                    eng.drain()
                emit("v", f11)
                o3v[t] = bump("v")
                def f12(eng, z=z, sh2=sh2, c2=c2, g2b=g2b):
                    eng.bn_stats(st12b[:, sh2:sh2 + 6], z[:, 0:512])
                    eng.bn_stats(st12b[:, sh2 + 6:sh2 + 12], z[:, 512:1024])
                    eng.drain()
                    eng.bn_aggr(ag4b[:, g2b:g2b + 2],
                                st12b[:, sh2:sh2 + 12])
                    eng.drain()
                    eng.tensor_scalar(sc[:, c2:c2 + 1],
                                      ag4b[:, g2b + 1:g2b + 2],
                                      1e-5, None, ALU.add)
                    eng.tensor_scalar(sc[:, c2 + 3:c2 + 4],
                                      ag4b[:, g2b:g2b + 1], -1.0, None,
                                      ALU.mult)
                    eng.drain()
                    eng.reciprocal(sc[:, c2 + 1:c2 + 2],
                                   sc[:, c2:c2 + 1]).then_inc(sem_v, 1)
                emit("v", f12)
                stats2v[t] = bump("v")

            def SD(t, gp):
                W("s", "v", stats2v[t])
                if gp >= 2:
                    _ok, _ov = outv[gp - 2]
                    W("s", _ok, _ov)
                c2 = 56 + 5 * (t % 2)
                def f15(eng, t=t, c2=c2, z=zq(t)):
                    eng.activation(sc[:, c2 + 2:c2 + 3],
                                   sc[:, c2 + 1:c2 + 2], AF.Sqrt)
                    eng.drain()
                    eng.activation(sc[:, c2 + 4:c2 + 5],
                                   sc[:, c2 + 3:c2 + 4], AF.Identity,
                                   scale=sc[:, c2 + 2:c2 + 3])
                    eng.drain()
                    eng.activation(
                        outst2[:, (t % 2) * 1024:(t % 2) * 1024 + 1024],
                        z[:, :], AF.Identity, bias=sc[:, c2 + 4:c2 + 5],
                        scale=sc[:, c2 + 2:c2 + 3]).then_inc(sem_s, 1)
                emit("s", f15)
                outrd[t] = bump("s")

            def G_out(t, gp):
                W("g", "s", outrd[t])
                osem_ = sem_o if (t % 2) == 0 else sem_o2
                okey = "o" if (t % 2) == 0 else "o2"
                def f16(eng, t=t, b=b, osem_=osem_):
                    dst = ap(y_h, b * S * D + t * 128 * D, (D, 128), (1, D))
                    eng.dma_start(dst,
                                  outst2[:, (t % 2) * 1024:
                                         (t % 2) * 1024 + 1024]
                                  ).then_inc(osem_, 16)
                emit("g", f16)
                bump(okey, 16)
                outv[gp] = (okey, ctr[okey])

            gps = {}
            for t in range(4):
                emit_wo(t)
            for t in range(2):
                gps[t] = emit_xdma(t)
                VA(t, *gps[t])
                SA(t)
                SB(t)
                VC(t)
                G_x1pb(t)
                T_qt_ffn(t)
            for t in (2, 3):
                gps[t] = emit_xdma(t)
                VA(t, *gps[t])
                SA(t)
            for i in range(NP):
                VD(i)
                if preload_cb is not None and i in (13, 15):
                    preload_cb(i, wo_y2v, ffnv)
                if i + 4 < NP:
                    emit_wo(i + 4)
                SD(i, gps[i][0])
                G_out(i, gps[i][0])
                if i + 2 < NP:
                    SB(i + 2)
                    VC(i + 2)
                    G_x1pb(i + 2)
                    T_qt_ffn(i + 2)
                if i + 4 < NP:
                    gps[i + 4] = emit_xdma(i + 4)
                    VA(i + 4, *gps[i + 4])
                    SA(i + 4)
            return ctr["v"]

        v_ready = ph0()
        gpair = [0, {}, {}]
        prev_t_end = 0
        prev_v_end = 0
        prev_s_end = 0
        pre = None
        for b in range(BPC):
            qk_s, vones, emit_vg, v_upto = ph1(b, v_ready, prev_t_end,
                                               prev_v_end, prev_s_end,
                                               pre)
            (trans_done, wo_done, w8_done, exp_last,
             trans_ev) = ph2(b, qk_s, vones, emit_vg, v_upto)

            nxt = {}
            def preload_cb(i, wo_y2v, ffnv, nxt=nxt, bn=b + 1):
                if i == 13:
                    W("g", "t", wo_y2v[15])
                    def fq(eng):
                        src = ap(wq_h, 0, (D, 128), (128 * D, 8), (1, D))
                        dst = ap(wslot[0], 0, (8192, 128), (1024, 8),
                                 (1, 1024))
                        eng.dma_start(dst, src).then_inc(sem_g, 16)
                    emit("g", fq)
                    nxt["wq"] = ("gd", bump("gd", 16))
                    W("g", "t", ffnv[13])
                    for c in range(4):
                        def fa(eng, c=c, bn=bn):
                            src = ap(xt_h, bn * D * S + (2 * c) * 128 * S,
                                     (S, 128), (128 * S, 2), (1, 1792))
                            dst = ap(xT, (2 * c) * 2048, (16384, 128),
                                     (2048, 2), (1, 1792))
                            eng.dma_start(dst, src).then_inc(sem_gx, 16)
                        emit("g", fa)
                        bump("gx", 16)
                else:
                    W("g", "t", ffnv[15])
                    def fk(eng):
                        src = ap(wk_h, 0, (D, 128), (128 * D, 8), (1, D))
                        dst = ap(wslot[1], 0, (8192, 128), (1024, 8),
                                 (1, 1024))
                        eng.dma_start(dst, src).then_inc(sem_gb, 16)
                    emit("g", fk)
                    nxt["wk"] = ("gb", bump("gb", 16))
                    for c in range(4):
                        def fb(eng, c=c, bn=bn):
                            src = ap(xt_h,
                                     bn * D * S + (2 * c) * 128 * S + 1792,
                                     (S, 128), (128 * S, 2), (1, 256))
                            dst = ap(xT, (2 * c) * 2048 + 1792,
                                     (16384, 128), (2048, 2), (1, 256))
                            eng.dma_start(dst, src).then_inc(sem_gx, 16)
                        emit("g", fb)
                        bump("gx", 16)
                    nxt["xt"] = ctr["gx"]

            cb = preload_cb if b + 1 < BPC else None
            ph45(b, trans_done, wo_done, w8_done, gpair, exp_last,
                 trans_ev, cb)
            pre = nxt if b + 1 < BPC else None
            prev_t_end = ctr["t"]
            prev_v_end = ctr["v"]
            prev_s_end = ctr["s"]

        to0, to1 = ctr["o"], ctr["o2"]
        def _final(eng):
            eng.wait_ge(sem_o, to0)
            eng.wait_ge(sem_o2, to1)
        emit("g", _final)

        @block.tensor
        def _(eng):
            for fn in ops["t"]:
                fn(eng)

        @block.vector
        def _(eng):
            for fn in ops["v"]:
                fn(eng)

        @block.scalar
        def _(eng):
            for fn in ops["s"]:
                fn(eng)

        @block.gpsimd
        def _(eng):
            for fn in ops["g"]:
                fn(eng)

        @block.sync
        def _(eng):
            for fn in ops["y"]:
                fn(eng)

    return nc


_NC_CACHE = None


def kernel(L0_attn_x, L0_attn_WQ, L0_attn_WK, L0_attn_WV, L0_attn_WO,
           L0_ffn_w_int8, L0_ffn_weight_scale, L0_ffn_bias):
    global _NC_CACHE
    x = np.ascontiguousarray(np.asarray(L0_attn_x, dtype=np.float32))
    xt = np.ascontiguousarray(np.swapaxes(x, 1, 2))
    wqt = np.ascontiguousarray(np.asarray(L0_attn_WQ, np.float32).T)
    wkt = np.ascontiguousarray(np.asarray(L0_attn_WK, np.float32).T)
    wvt = np.ascontiguousarray(np.asarray(L0_attn_WV, np.float32).T)
    wot = np.ascontiguousarray(np.asarray(L0_attn_WO, np.float32).T)
    w8t = np.ascontiguousarray(np.asarray(L0_ffn_w_int8, np.int32).T)
    wsc = np.asarray(L0_ffn_weight_scale, np.float32).reshape(1, 1)
    bias = np.asarray(L0_ffn_bias, np.float32).reshape(1, D)

    if _NC_CACHE is None:
        _NC_CACHE = build_nc()
    nc = _NC_CACHE

    in_maps = []
    for c in range(NCORE):
        in_maps.append({
            "x": np.ascontiguousarray(x[c * BPC:(c + 1) * BPC]),
            "xt": np.ascontiguousarray(xt[c * BPC:(c + 1) * BPC]),
            "wqt": wqt, "wkt": wkt, "wvt": wvt, "wot": wot,
            "w8t": w8t, "wsc": wsc, "bias": bias,
        })
    res = run_bass_kernel_spmd(nc, in_maps, core_ids=list(range(NCORE)))
    out = np.concatenate([np.asarray(r["y"]) for r in res.results], axis=0)
    return np.ascontiguousarray(out.astype(np.float32))
